# version 13
# baseline (speedup 1.0000x reference)
import numpy as np
from contextlib import ExitStack
import concourse.bass as bass
import concourse.mybir as mybir
from concourse.bass_utils import run_bass_kernel_spmd

F32 = mybir.dt.float32
BF16 = mybir.dt.bfloat16
AX = mybir.AxisListType.X
ALU = mybir.AluOpType
ACT = mybir.ActivationFunctionType

B, L, DM, DI, DS, FF = 2, 16384, 1024, 2048, 16, 4096
TOK, TT = 4096, 512
NTILE = TOK // TT
CH = 64
NCH = TT // CH
AUGK = CH + DS
DMC, DIC, FFC = DM // 128, DI // 128, FF // 128
C23 = 12582912.0
C52 = 6755399441055744.0
PROBE = 1.5 * 2.0 ** -30
EPS = 1e-5


class Prog:
    ENGINES = ("sync", "vector", "scalar", "tensor", "gpsimd")

    def __init__(self, nc, ctx):
        self.nc, self.ctx = nc, ctx
        self.q = {e: [] for e in self.ENGINES}
        self.sems, self.cnt = {}, {}
        self.w = {e: [] for e in self.ENGINES}
        self.r = {e: [] for e in self.ENGINES}
        self.maxwait = {}
        self.synced = {}

    def sem(self, name):
        if name not in self.sems:
            self.sems[name] = self.ctx.enter_context(self.nc.semaphore(name))
            self.cnt[name] = 0
        return self.sems[name]

    def emit(self, eng, fn):
        self.q[eng].append(fn)

    @staticmethod
    def _foot(ap):
        try:
            t = ap.tensor
            name = t.name
            w = int(t.shape[-1])
            off = int(ap.offset)
            p0, c0 = off // w, off % w
            pext = 0
            cext = 0
            for s, n in ap.ap:
                if n <= 1:
                    continue
                if s > 0 and s % w == 0:
                    pext += (n - 1) * (s // w)
                else:
                    cext += (n - 1) * s
            return (name, p0, p0 + pext, c0, c0 + cext)
        except Exception:
            try:
                return (ap.name, 0, 1 << 30, 0, 1 << 30)
            except Exception:
                return None

    @staticmethod
    def _ovl(a, b):
        return (a[0] == b[0] and a[1] <= b[2] and b[1] <= a[2]
                and a[3] <= b[4] and b[3] <= a[4])

    def _predrain(self, eng, outs, ins):
        ni = [f for f in (self._foot(a) for a in ins) if f]
        no = [f for f in (self._foot(a) for a in outs) if f]
        hit = any(self._ovl(i, w) for i in ni for w in self.w[eng]) or \
            any(self._ovl(o, x) for o in no
                for x in (self.w[eng] + self.r[eng]))
        if hit or len(self.w[eng]) + len(self.r[eng]) > 512:
            self.emit(eng, lambda e: e.drain())
            self.w[eng], self.r[eng] = [], []
        self.w[eng] += no
        self.r[eng] += ni

    def _fin(self, eng, fn, inc):
        if inc is None:
            self.emit(eng, fn)
        else:
            h = self.sem(inc)
            self.cnt[inc] += 1
            self.emit(eng, lambda e: fn(e).then_inc(h, 1))

    def op(self, eng, outs, ins, fn, inc=None):
        self._predrain(eng, outs, ins)
        self._fin(eng, fn, inc)

    def wait(self, eng, sem, val=None):
        h = self.sem(sem)
        v = self.cnt[sem] if val is None else val
        if v > 0:
            self.emit(eng, lambda e: e.wait_ge(h, v))
            self.maxwait[sem] = max(self.maxwait.get(sem, 0), v)
            k = (eng, sem)
            self.synced[k] = max(self.synced.get(k, 0), v)

    def dma(self, eng, out, in_, sem, transpose=False):
        h = self.sem(sem)
        mw = self.maxwait.get(sem, 0)
        if self.synced.get((eng, sem), 0) < mw:
            self.wait(eng, sem, mw)
        self.cnt[sem] += 16
        self.emit(eng, lambda e: e.dma_start(
            out=out, in_=in_, transpose=transpose).then_inc(h, 16))

    def ts(self, eng, out, in0, s1, s2, op0, op1=None, inc=None):
        ins = [in0] + [s for s in (s1, s2) if hasattr(s, "tensor")]
        if op1 is None:
            self.op(eng, [out], ins,
                    lambda e: e.tensor_scalar(out, in0, s1, None, op0), inc)
        else:
            self.op(eng, [out], ins,
                    lambda e: e.tensor_scalar(out, in0, s1, s2, op0, op1), inc)

    def tt(self, eng, out, a, b, alu, inc=None):
        self.op(eng, [out], [a, b], lambda e: e.tensor_tensor(out, a, b, alu), inc)

    def stt(self, eng, out, in0, s, in1, op0, op1, inc=None):
        ins = [in0, in1] + ([s] if hasattr(s, "tensor") else [])
        self.op(eng, [out], ins,
                lambda e: e.scalar_tensor_tensor(out, in0, s, in1, op0, op1), inc)

    def scan(self, eng, out, d0, d1, init, op0, op1, inc=None):
        ins = [d0, d1] + ([init] if hasattr(init, "tensor") else [])
        self.op(eng, [out], ins,
                lambda e: e.tensor_tensor_scan(out, d0, d1, init, op0, op1), inc)

    def red(self, eng, out, in_, alu, absval=False, inc=None, axis=AX):
        self.op(eng, [out], [in_],
                lambda e: e.tensor_reduce(out, in_, axis=axis, op=alu,
                                          apply_absolute_value=absval), inc)

    def cp(self, eng, out, in_, inc=None):
        self.op(eng, [out], [in_], lambda e: e.tensor_copy(out, in_), inc)

    def act(self, eng, out, in_, func, bias=0.0, scale=1.0, accum=None, inc=None):
        ins = [in_] + [a for a in (bias, scale) if hasattr(a, "tensor")]
        outs = [out] + ([accum] if accum is not None else [])
        if accum is None:
            self.op(eng, outs, ins, lambda e: e.activation(
                out, in_, func, bias=bias, scale=scale), inc)
        else:
            self.op(eng, outs, ins, lambda e: e.activation(
                out, in_, func, bias=bias, scale=scale, accum_out=accum), inc)

    def recip(self, eng, out, in_, inc=None):
        self.op(eng, [out], [in_], lambda e: e.reciprocal(out, in_), inc)

    def mset(self, eng, ap, val, inc=None):
        self.op(eng, [ap], [], lambda e: e.memset(ap, val), inc)

    def mm(self, out, lhsT, rhs, start, stop, tp=None, inc=None):
        self._fin("tensor", lambda e: e.matmul(out, lhsT, rhs, start=start,
                                               stop=stop, tile_position=tp), inc)

    def nopinc(self, eng, sem):
        h = self.sem(sem)
        self.cnt[sem] += 1
        if eng == "sync":
            self.emit(eng, lambda e: e.nop().then_inc(h, 1))
        else:
            self.emit(eng, lambda e: e.drain().then_inc(h, 1))
            self.w[eng], self.r[eng] = [], []

    def barrier(self, engines, sem):
        for e in engines:
            self.nopinc(e, sem)
        tgt = self.cnt[sem]
        for e in engines:
            self.wait(e, sem, tgt)

    def flush(self):
        with self.nc.Block() as block:
            for name in self.ENGINES:
                ops = self.q[name]
                if not ops:
                    continue
                def body(e, ops=ops):
                    for f in ops:
                        f(e)
                getattr(block, name)(body)
        self.q = {e: [] for e in self.ENGINES}
        self.w = {e: [] for e in self.ENGINES}
        self.r = {e: [] for e in self.ENGINES}


def build_nc(n_tiles=NTILE, sim_cc=False, dbg=False):
    nc = bass.Bass("TRN2", target_bir_lowering=False, debug=False)
    ctx = ExitStack()

    dram = lambda n, s, d, k="ExternalInput": nc.dram_tensor(n, s, d, kind=k).ap()
    XT = dram("xt", [DM, TOK + 3], F32)
    WIN = dram("w_inT", [DM, 2 * DI], F32)
    WX = dram("wxT", [DI, 33], F32)
    CONVW = dram("convw", [128, 4 * DIC], F32)
    CONVB = dram("convb", [128, DIC], F32)
    DPAR = dram("dparam", [128, DIC], F32)
    ALOG = dram("alog48", [48, 1], F32)
    WOUT = dram("w_outT", [DI, DM], F32)
    WGU = dram("wguT", [DM, 2 * FF], F32)
    WD = dram("wdT", [FF, DM], F32)
    MASK = dram("mask", [CH, AUGK], F32)
    I16 = dram("ident16", [16, 16], F32)
    CMB = dram("cmb", [16, 9], F32)
    OUT = dram("out", [TOK, DM], F32, "ExternalOutput")

    kindd = "ExternalOutput" if dbg else "Internal"
    SP_XC = dram("sp_xc", [DI, TOK], BF16, kindd)
    SP_Z = dram("sp_z", [DI, TOK], BF16, kindd)
    SP_Y = dram("sp_y", [TOK, DI], BF16, kindd)
    SP_RES = dram("sp_res", [DM, TOK], BF16, kindd)
    SP_X1 = dram("sp_x1", [TOK, DM], BF16, kindd)
    SP_AQ = dram("sp_aq", [TOK, DM], BF16, kindd)
    SP_AQ2 = dram("sp_aq2", [TOK, FF], BF16, kindd)
    SP_QGU = dram("sp_qgu", [DM, 2 * FF], BF16, kindd)
    SP_QD = dram("sp_qd", [FF, DM], BF16, kindd)
    CC_IN = dram("cc_in", [16, DI + 1], F32, "Internal")
    CC_OUT = dram("cc_out", [64, DI + 1], F32, "Internal")

    XTr = XT.rearrange("(c p) t -> p c t", p=128)
    WINr = WIN.rearrange("(c p) f -> c p f", p=128)
    WXr = WX.rearrange("(c p) f -> p c f", p=128)
    WOUTr = WOUT.rearrange("(c p) f -> p c f", p=128)
    WGUr = WGU.rearrange("(c p) f -> c p f", p=128)
    WDr = WD.rearrange("(c p) f -> c p f", p=128)
    QGUr = SP_QGU.rearrange("(c p) f -> c p f", p=128)
    QDr = SP_QD.rearrange("(c p) f -> c p f", p=128)
    XCr = SP_XC.rearrange("(c p) t -> p c t", p=128)
    Zr_ = SP_Z.rearrange("(c p) t -> p c t", p=128)
    RESr = SP_RES.rearrange("(c p) t -> p c t", p=128)

    sbuf = lambda n, s, d: ctx.enter_context(nc.sbuf_tensor(n, s, d))
    psum = lambda n, s: ctx.enter_context(nc.psum_tensor(n, s, F32))

    convw_sb = sbuf("convw_sb", [128, 4 * DIC], F32)
    convb_sb = sbuf("convb_sb", [128, DIC], F32)
    dpar_sb = sbuf("dpar_sb", [128, DIC], F32)
    acol = sbuf("acol", [48, 1], F32)
    onecol = sbuf("onecol", [65, 1], F32)
    onesf = sbuf("onesf", [65, 128], F32)
    cprobe = sbuf("cprobe", [128, 1], F32)
    cmag = sbuf("cmag", [128, 1], F32)
    cmagn = sbuf("cmagn", [128, 1], F32)
    epscol = sbuf("epscol", [128, 1], F32)
    nhalf = sbuf("nhalf", [128, 1], F32)
    maskb = sbuf("maskb", [CH, AUGK], F32)
    i16b = sbuf("i16b", [16, 16], BF16)
    cmb_sb = sbuf("cmb_sb", [16, 9], F32)
    wxb = sbuf("wxb", [128, DIC * 33], BF16)
    ppcol = sbuf("ppcol", [16, 1], F32)
    invsa = sbuf("invsa", [128, 32], F32)
    invs2 = sbuf("invs2", [128, 32], F32)
    wsum = sbuf("wsum", [128, 8], F32)
    wmean = sbuf("wmean", [128, 4], F32)
    swcol = sbuf("swcol", [128, 4], F32)
    xmb = sbuf("xmb", [128, 24], BF16)
    stat = sbuf("stat", [128, 16], F32)
    kcol = sbuf("kcol", [128, 8], F32)
    qcol = sbuf("qcol", [128, 3], F32)
    lnst = sbuf("lnst", [128, 48], F32)

    phxs = ExitStack()
    wstg = phxs.enter_context(nc.sbuf_tensor("wstg", [128, 2 * 1024], F32))
    wscr = phxs.enter_context(nc.sbuf_tensor("wscr", [128, 1024], F32))
    wq2 = phxs.enter_context(nc.sbuf_tensor("wq2", [128, 2 * 1024], BF16))

    ph12 = ExitStack()
    s12 = lambda n, s, d: ph12.enter_context(nc.sbuf_tensor(n, s, d))
    cgbuf = s12("cgbuf", [16, TOK], BF16)
    hin = s12("hin", [16, DI], F32)
    hinb = s12("hinb", [16, DI], BF16)
    xbf = s12("xbf", [128, DMC * TT], BF16)

    p_a = psum("p_a", [128, TT])
    p_b = psum("p_b", [128, TT])

    p = Prog(nc, ctx)
    LD, SPL, TR, SS, GP = "s_ld", "s_spl", "s_tr", "s_ss", "s_gp"
    TE, VE, AC, GC = "s_te", "s_ve", "s_ac", "s_gc"
    EVZV, EVZA = "s_evzv", "s_evza"
    BAR, STO, GW = "s_bar", "s_sto", "s_gw"
    SPLZ = ("s_splz0", "s_splz1")
    SSB = "s_ssb"
    for s in (LD, SPL, TR, SS, SSB, GP, TE, VE, AC, GC, EVZV, EVZA, BAR, STO,
              GW) + SPLZ:
        p.sem(s)

    ph1 = ExitStack()
    sb1 = lambda n, s, d: ph1.enter_context(nc.sbuf_tensor(n, s, d))
    ps1 = lambda n, s: ph1.enter_context(nc.psum_tensor(n, s, F32))
    hpack = sb1("hpack", [16, DI + 1], F32)
    p_x = ps1("p_x", [16, TT])
    p_dt2 = ps1("p_dt2", [16, TT])
    p_g = ps1("p_g", [128, TT])
    p_h = ps1("p_h", [128, TT])
    p_m1 = ps1("p_m1", [CH, AUGK])
    p_m2 = ps1("p_m2", [AUGK, AUGK])
    winb = sb1("winb", [128, DMC * 2 * DI], BF16)
    xproj = sb1("xproj", [128, DIC * (TT + 3)], BF16)
    xconv = sb1("xconv", [128, DIC * TT], BF16)
    zsilc = sb1("zsilc", [128, 2 * TT], BF16)
    uw = sb1("uw", [128, 2 * DI], BF16)
    rhsa = sb1("rhsa", [AUGK, 2 * DI], BF16)
    ysb = sb1("ysb", [CH, 2 * DI], BF16)
    dta = sb1("dta", [16, TT], F32)
    s16 = sb1("s16", [16, TT], F32)
    dtsb = sb1("dtsb", [16, TT], F32)
    braw = sb1("braw", [16, TT], F32)
    bbar = sb1("bbar", [16, TT], F32)
    cevac = sb1("cevac", [16, TT], F32)
    fbc = sb1("fbc", [16, TT], F32)
    eargB = sb1("eargB", [16, TT], F32)
    eargC = sb1("eargC", [16, TT], F32)
    bpp = sb1("bpp", [16, TT], BF16)
    cpk = sb1("cpk", [16, TT], BF16)
    ccb = sb1("ccb", [16, 24], F32)
    ecb = sb1("ecb", [16, 24], F32)
    sp8 = sb1("sp8", [16, 8], F32)
    diag2a = sb1("diag2a", [16, 8 * 16], BF16)
    rhsm = sb1("rhsm", [16, NCH * AUGK], BF16)
    taug = sb1("taug", [AUGK, 2 * AUGK], BF16)
    dtch = sb1("dtch", [1, TT], F32)

    p.dma("sync", convw_sb[:], CONVW, LD)
    p.dma("sync", convb_sb[:], CONVB, LD)
    p.dma("sync", dpar_sb[:], DPAR, LD)
    p.dma("sync", acol[:], ALOG, LD)
    p.dma("sync", maskb[:], MASK, LD)
    p.dma("sync", cmb_sb[:], CMB, LD)
    p.dma("sync", hpack[0:16, 0:16], I16, LD)
    for c in range(DMC):
        p.dma("gpsimd", winb[:, c * 4096:(c + 1) * 4096], WINr[c], GW)
    p.dma("gpsimd", wxb[:].rearrange("p (c f) -> p c f", c=DIC), WXr, GW)
    p.dma("gpsimd", xmb[:].rearrange("p (c t) -> p c t", c=8),
          XTr[:, :, 0:3], GW)
    gw_w = p.cnt[GW]

    p.wait("vector", LD)
    p.mset("vector", cprobe[:], PROBE)
    p.ts("vector", cprobe[:], cprobe[:], C23, -C23, ALU.add, ALU.add)
    p.ts("vector", cprobe[:], cprobe[:], 536870912.0, None, ALU.mult)
    p.ts("vector", cmag[:], cprobe[:], -C23, C23, ALU.mult, ALU.add)
    p.stt("vector", cmag[:], cprobe[:], C52, cmag[:], ALU.mult, ALU.add)
    p.ts("vector", cmagn[:], cmag[:], -1.0, None, ALU.mult)
    p.mset("vector", epscol[:], EPS)
    p.mset("vector", qcol[:, 0:1], 1.0)
    p.mset("vector", qcol[:, 1:2], 2.0)
    p.mset("vector", qcol[:, 2:3], -1.0)
    p.mset("vector", nhalf[:], -0.5)
    p.mset("vector", onecol[:], 1.0)
    p.mset("vector", onesf[:], 1.0)
    p.mset("vector", ppcol[:], 1.0)
    p.mset("vector", kcol[:, 0:1], 1.0)
    p.cp("vector", i16b[:], hpack[0:16, 0:16])
    p.mset("vector", hpack[:], 0.0)
    p.mset("vector", rhsa[64:80, 0:DI], 0.0)
    p.nopinc("vector", VE)
    ve_consts = p.cnt[VE]
    p.wait("scalar", LD, None)
    p.act("scalar", acol[:], acol[:], ACT.Exp, inc=AC)
    p.wait("vector", AC, p.cnt[AC])
    p.ts("vector", acol[:], acol[:], -1.0, None, ALU.mult, inc=VE)
    ve_acol = p.cnt[VE]

    wq_chunks = []
    for c in range(DMC):
        for j in range(8):
            col = j * 1024
            wq_chunks.append((WGUr[c][:, col:col + 1024], 1024, j // 4))
    for c in range(FFC):
        wq_chunks.append((WDr[c], 1024, 2))
    wq_marks = {}
    wq_first = [True, True, True]

    def emit_wq_batch(lo, hi):
        for k in range(lo, hi):
            src_ap, width, tgt = wq_chunks[k]
            par = k % 2
            gsl = wstg[:, par * 1024: par * 1024 + width]
            if k >= 2:
                p.wait("gpsimd", VE, wq_marks[k - 2])
            else:
                p.wait("gpsimd", VE, ve_consts)
            p.dma("gpsimd", gsl, src_ap, GP)
            p.wait("vector", GP, None)
            p.red("vector", wsum[:, 4:5], gsl, ALU.add, absval=True)
            if wq_first[tgt]:
                p.cp("vector", wsum[:, tgt:tgt + 1], wsum[:, 4:5], inc=VE)
                wq_first[tgt] = False
            else:
                p.tt("vector", wsum[:, tgt:tgt + 1], wsum[:, tgt:tgt + 1],
                     wsum[:, 4:5], ALU.add, inc=VE)
            wq_marks[k] = p.cnt[VE]

    gu_chunks = []
    for c in range(DMC):
        for j in range(8):
            col = j * 1024
            gu_chunks.append((WGUr[c][:, col:col + 1024],
                              QGUr[c][:, col:col + 1024], j // 4))
    wd_chunks = [(WDr[c], QDr[c], 2) for c in range(FFC)]
    all_chunks = gu_chunks + wd_chunks
    qac_marks = {}

    def _quant_load(ch, base, i):
        k = base + i
        if k - 2 in qac_marks:
            p.wait("gpsimd", AC, qac_marks[k - 2])
        p.dma("gpsimd", wstg[:, (k % 2) * 1024:(k % 2) * 1024 + 1024],
              ch[i][0], GP)

    def emit_quant_tile(ch, base):
        for i in range(min(2, len(ch))):
            _quant_load(ch, base, i)
        for i in range(len(ch)):
            k = base + i
            par = k % 2
            sl = wstg[:, par * 1024:par * 1024 + 1024]
            ql = wq2[:, par * 1024:par * 1024 + 1024]
            p.wait("scalar", GP)
            p.act("scalar", wscr[:], sl, ACT.Identity, bias=cmag[:, 0:1],
                  scale=swcol[:, ch[i][2]:ch[i][2] + 1])
            p.act("scalar", sl, wscr[:], ACT.Identity, bias=cmagn[:, 0:1])
            p.act("scalar", wscr[:], sl, ACT.Relu, bias=qcol[:, 0:1])
            p.act("scalar", sl, wscr[:], ACT.Relu, bias=qcol[:, 1:2],
                  scale=qcol[:, 2:3])
            p.act("scalar", ql, sl, ACT.Identity,
                  bias=qcol[:, 0:1], scale=qcol[:, 2:3], inc=AC)
            qac_marks[k] = p.cnt[AC]
            p.wait("gpsimd", AC, qac_marks[k])
            p.dma("gpsimd", ch[i][1], ql, GP)
            if i + 2 < len(ch):
                _quant_load(ch, base, i + 2)

    ve_wbc = 0
    nwq = min(2, n_tiles)
    pb1 = {"a": None, "b": None, "g": None, "h": None}
    te_aug_end = [0]

    for t in range(n_tiles):
        T0 = t * TT
        if t > 0:
            p.wait("gpsimd", TE, te_xz)
        p.dma("gpsimd", xbf[:].rearrange("p (c t) -> p c t", c=DMC),
              XTr[:, :, 3 + T0:3 + T0 + TT], GW)
        if t > 0:
            p.op("vector", [xproj[:]], [xproj[:]], lambda e: e.tensor_copy(
                xproj[:].rearrange("p (c m) -> p c m", c=DIC)[:, :, 0:3],
                xproj[:].rearrange("p (c m) -> p c m", c=DIC)[:, :, TT:TT + 3]))
            p.nopinc("vector", VE)
        ve_x = p.cnt[VE]

        p.wait("tensor", VE, ve_x)
        p.wait("tensor", GW, None)
        ac_xp = {}
        for fc in range(32):
            bk = ("a", "b")[fc % 2]
            bank = (p_a, p_b)[fc % 2]
            if pb1[bk] is not None:
                p.wait("tensor", *pb1[bk])
            for dm in range(DMC):
                lw = winb[:, dm * 4096 + fc * 128: dm * 4096 + (fc + 1) * 128]
                p.mm(bank[:], lw, xbf[:, dm * TT:(dm + 1) * TT],
                     start=(dm == 0), stop=(dm == DMC - 1),
                     inc=(TE if dm == DMC - 1 else None))
            te_fc = p.cnt[TE]
            te_xz = te_fc
            if fc < 16:
                if fc == 0:
                    p.wait("scalar", VE, ve_x)
                p.wait("scalar", TE, te_fc)
                p.act("scalar", xproj[:, fc * (TT + 3) + 3:
                                      (fc + 1) * (TT + 3)],
                      bank[:], ACT.Identity, inc=EVZA)
                pb1[bk] = (EVZA, p.cnt[EVZA])
                ac_xp[fc] = (EVZA, p.cnt[EVZA])
            else:
                c = fc - 16
                par = c % 2
                p.wait("scalar", SPLZ[par], None)
                p.wait("scalar", TE, te_fc)
                p.act("scalar", zsilc[:, par * TT:(par + 1) * TT], bank[:],
                      ACT.Silu, inc=EVZA)
                pb1[bk] = (EVZA, p.cnt[EVZA])
                p.wait("sync", EVZA, None)
                p.dma("sync", Zr_[:, c:c + 1, T0:T0 + TT],
                      zsilc[:, par * TT:(par + 1) * TT]
                      .rearrange("p (c t) -> p c t", c=1), SPLZ[par])

        if t == 0:
            for bb in ("a", "b"):
                if pb1[bb] is not None:
                    p.wait("tensor", *pb1[bb])
            for fc in range(16):
                for dm in range(DMC):
                    lw = winb[:, dm * 4096 + fc * 128: dm * 4096 + (fc + 1) * 128]
                    p.mm(p_a[:, fc * 3:(fc + 1) * 3], lw,
                         xmb[:, dm * 3:(dm + 1) * 3],
                         start=(fc == 0 and dm == 0),
                         stop=(fc == 15 and dm == DMC - 1))
            p.nopinc("tensor", TE)
            p.wait("vector", TE, p.cnt[TE])
            p.op("vector", [xproj], [p_a], lambda e: e.tensor_copy(
                xproj[:].rearrange("p (c m) -> p c m", c=DIC)[:, :, 0:3],
                p_a[:, 0:48].rearrange("p (c m) -> p c m", c=16)), inc=VE)
            pb1["a"] = (VE, p.cnt[VE])
        ve_mgn = p.cnt[VE]

        p.wait("vector", VE, ve_mgn)
        ve_conv = {}
        for k in (0, 1, 2, 3):
            for dc in range(DIC):
                base = dc * (TT + 3)
                dst = xconv[:, dc * TT:(dc + 1) * TT]
                if k == 0:
                    p.wait("vector", *ac_xp[dc])
                    p.ts("vector", dst, xproj[:, base:base + TT],
                         convw_sb[:, dc * 4:dc * 4 + 1], convb_sb[:, dc:dc + 1],
                         ALU.mult, ALU.add)
                else:
                    p.stt("vector", dst, xproj[:, base + k:base + k + TT],
                          convw_sb[:, dc * 4 + k:dc * 4 + k + 1], dst,
                          ALU.mult, ALU.add, inc=(VE if k == 3 else None))
                if k == 3:
                    ve_conv[dc] = p.cnt[VE]
        ac_sil = {}
        for dc in range(DIC):
            p.wait("scalar", VE, ve_conv[dc])
            sl = xconv[:, dc * TT:(dc + 1) * TT]
            p.act("scalar", sl, sl, ACT.Silu, inc=AC)
            ac_sil[dc] = p.cnt[AC]
        ac_conv = p.cnt[AC]

        p.wait("sync", AC, ac_conv)
        p.dma("sync", XCr[:, :, T0:T0 + TT],
              xconv[:].rearrange("p (c t) -> p c t", c=DIC), SPL)
        spl_xc = p.cnt[SPL]

        if t == nwq and ve_wbc:
            p.wait("tensor", VE, ve_wbc)
        for dc in range(DIC):
            p.wait("tensor", AC, ac_sil[dc])
            rhs = xconv[:, dc * TT:(dc + 1) * TT]
            p.mm(p_x[:], wxb[:, dc * 33:dc * 33 + 16], rhs,
                 start=(dc == 0), stop=(dc == DIC - 1),
                 inc=(TE if dc == DIC - 1 else None))
        te_B = p.cnt[TE]
        for dc in range(DIC):
            rhs = xconv[:, dc * TT:(dc + 1) * TT]
            p.mm(p_dt2[0:1, :], wxb[:, dc * 33 + 32:dc * 33 + 33], rhs,
                 start=(dc == 0), stop=(dc == DIC - 1),
                 inc=(TE if dc == DIC - 1 else None))
        te_dt = p.cnt[TE]

        p.wait("vector", TE, te_B)
        p.cp("vector", braw[:], p_x[:], inc=VE)
        ve_braw = p.cnt[VE]
        p.wait("vector", TE, te_dt)
        p.ts("vector", dtch[0:1, :], p_dt2[0:1, :], 0.0, None, ALU.min,
             inc=VE)
        p.wait("scalar", VE, p.cnt[VE])
        p.act("scalar", dtch[0:1, :], dtch[0:1, :], ACT.Exp)
        p.act("scalar", dtch[0:1, :], dtch[0:1, :], ACT.Ln,
              bias=onecol[0:1, 0:1], inc=AC)
        p.wait("vector", AC, p.cnt[AC])
        p.ts("vector", dtch[0:1, :], dtch[0:1, :], 0.1, 0.001,
             ALU.min, ALU.max, inc=VE)
        ve_dtc = p.cnt[VE]

        p.wait("tensor", VE, ve_braw)
        for dc in range(DIC):
            rhs = xconv[:, dc * TT:(dc + 1) * TT]
            p.mm(p_x[:], wxb[:, dc * 33 + 16:dc * 33 + 32], rhs,
                 start=(dc == 0), stop=(dc == DIC - 1),
                 inc=(TE if dc == DIC - 1 else None))
        te_C = p.cnt[TE]
        p.wait("tensor", VE, ve_dtc)
        p.mm(p_dt2[:], onesf[0:1, 0:16], dtch[0:1, :], True, True, inc=TE)
        te_bc = p.cnt[TE]

        p.wait("vector", TE, te_bc)
        if t == 0:
            p.wait("vector", VE, ve_acol)
        p.cp("vector", dtsb[:], p_dt2[:])
        p.ts("vector", dta[:], p_dt2[:], acol[0:16, 0:1], -20.0,
             ALU.mult, ALU.max)
        p.tt("vector", bbar[:], braw[:], dtsb[:], ALU.mult)
        p.ts("vector", bbar[:], bbar[:], 10.0, -10.0, ALU.min, ALU.max)
        p.wait("vector", TE, te_C)
        p.cp("vector", cevac[:], p_x[:])
        p.scan("vector", s16[:], dta[:], dta[:], 0.0, ALU.add, ALU.bypass)
        p.op("vector", [fbc], [s16], lambda e: e.tensor_copy(
            fbc[:].rearrange("p (c t) -> p c t", c=NCH),
            s16[:, 31::64].unsqueeze(2).broadcast_to((16, NCH, CH))))
        p.mset("vector", sp8[:, 0:1], 0.0)
        p.op("vector", [sp8], [s16], lambda e: e.tensor_copy(
            sp8[:, 1:8], s16[:, 63::64][:, 0:7]))
        p.tt("vector", ccb[:, 0:8], s16[:, 63::64], s16[:, 31::64],
             ALU.subtract)
        p.tt("vector", ccb[:, 8:16], s16[:, 31::64], sp8[:], ALU.subtract)
        p.tt("vector", ccb[:, 16:24], s16[:, 63::64], sp8[:], ALU.subtract)
        p.tt("vector", eargB[:], fbc[:], s16[:], ALU.subtract)
        p.ts("vector", eargB[:], eargB[:], 87.0, -20.10126823437577,
             ALU.min, ALU.add)
        p.tt("vector", eargC[:], s16[:], fbc[:], ALU.subtract)
        p.ts("vector", eargC[:], eargC[:], 87.0, None, ALU.min)
        p.nopinc("vector", VE)
        p.wait("scalar", VE, p.cnt[VE])
        p.act("scalar", ecb[:], ccb[:], ACT.Exp)
        p.act("scalar", eargB[:], eargB[:], ACT.Exp)
        p.act("scalar", eargC[:], eargC[:], ACT.Exp, inc=AC)
        p.wait("vector", AC, p.cnt[AC])
        p.tt("vector", bpp[:], bbar[:], eargB[:], ALU.mult)
        p.tt("vector", cpk[:], cevac[:], eargC[:], ALU.mult)
        p.op("vector", [rhsm], [cpk], lambda e: e.tensor_copy(
            rhsm[:].rearrange("p (c k) -> p c k", k=AUGK)[:, :, 0:CH],
            cpk[:].rearrange("p (c t) -> p c t", c=NCH)))
        for c in range(NCH):
            p.ts("vector", rhsm[:, c * AUGK + CH:(c + 1) * AUGK], i16b[:],
                 ecb[:, c:c + 1], None, ALU.mult)
            p.ts("vector", diag2a[:, c * 16:(c + 1) * 16], i16b[:],
                 ecb[:, 8 + c:9 + c], None, ALU.mult)
        for c in range(NCH):
            p.tt("vector", stat[0:16, 8:9], ecb[:, 8 + c:9 + c], ppcol[:],
                 ALU.mult)
            p.ts("vector", cgbuf[:, T0 + c * CH:T0 + (c + 1) * CH],
                 cpk[:, c * CH:(c + 1) * CH], stat[0:16, 8:9], None, ALU.mult)
            p.ts("vector", ppcol[:], ppcol[:], ecb[:, 16 + c:17 + c],
                 None, ALU.mult)
        p.nopinc("vector", VE)
        ve_prep = p.cnt[VE]

        if n_tiles == NTILE and 2 <= t:
            p.wait("scalar", VE, ve_wbc)
            p.wait("gpsimd", VE, ve_wbc)
            emit_quant_tile(all_chunks[8 * (t - 2):8 * (t - 1)], 8 * (t - 2))

        def issue_reload(w):
            p.wait("sync", SPL, None)
            p.wait("sync", SS, None)
            p.wait("sync", SSB, None)
            p.dma("sync", uw[:, (w % 2) * DI:(w % 2) * DI + DI],
                  SP_XC[:, T0 + w * 128:T0 + w * 128 + 128], TR,
                  transpose=True)

        def issue_copy(c):
            gc_ = t * NCH + c
            pr = gc_ % 2
            p.wait("sync", TR, None)
            if te_aug_end[0]:
                p.wait("sync", TE, te_aug_end[0])
            p.dma("sync", rhsa[0:64, pr * DI:(pr + 1) * DI],
                  uw[(c % 2) * 64:(c % 2) * 64 + 64,
                     ((c // 2) % 2) * DI:((c // 2) % 2) * DI + DI],
                  SS if pr == 0 else SSB)

        def emit_mb(c, first):
            if first:
                p.wait("tensor", VE, ve_prep)
            else:
                p.wait("tensor", VE, ve_taug[c - 1])
            p.mm(p_m1[:, :], bpp[:, c * CH:(c + 1) * CH],
                 rhsm[:, c * AUGK:(c + 1) * AUGK], True, True, tp=(0, 0))
            p.mm(p_m2[64:80, :], diag2a[:, c * 16:(c + 1) * 16],
                 rhsm[:, c * AUGK:(c + 1) * AUGK], True, True, tp=(0, 64),
                 inc=TE)
            te_mb[c] = p.cnt[TE]

        ve_taug = {}
        te_mb = {}
        emit_mb(0, True)
        issue_reload(0)
        issue_copy(0)
        issue_reload(1)
        for c in range(NCH):
            w0 = c * CH
            gci = t * NCH + c
            par, npar = gci % 2, (gci + 1) % 2

            tb = (c % 2) * AUGK
            p.wait("vector", TE, te_mb[c])
            p.tt("vector", taug[0:64, tb:tb + AUGK], p_m1[:, :], maskb[:],
                 ALU.mult)
            p.cp("vector", taug[64:80, tb:tb + AUGK], p_m2[64:80, :], inc=VE)
            ve_taug[c] = p.cnt[VE]

            if c + 1 < NCH:
                issue_copy(c + 1)
            if c % 2 == 1 and (c + 3) // 2 < NCH // 2:
                issue_reload((c + 3) // 2)
            if c + 1 < NCH:
                emit_mb(c + 1, False)

            p.wait("tensor", VE, ve_taug[c])
            p.wait("tensor", SS if par == 0 else SSB, None)
            p.wait("vector", SPL, None)
            yb = par * DI
            for n in range(4):
                bk = ("g", "h", "a", "b")[n]
                bank = (p_g, p_h, p_a, p_b)[n]
                if pb1[bk] is not None:
                    p.wait("tensor", *pb1[bk])
                p.mm(bank[0:80, :], taug[:, tb:tb + AUGK],
                     rhsa[:, par * DI + n * 512:par * DI + (n + 1) * 512],
                     True, True, inc=TE)
                p.wait("vector", TE, p.cnt[TE])
                p.ts("vector", rhsa[64:80, npar * DI + n * TT:
                                    npar * DI + (n + 1) * TT],
                     bank[64:80, :], 100.0, -100.0, ALU.min, ALU.max)
                p.cp("vector", ysb[:, yb + n * TT:yb + (n + 1) * TT],
                     bank[0:64, :], inc=VE)
                pb1[bk] = (VE, p.cnt[VE])
            te_aug_end[0] = p.cnt[TE]
            p.wait("sync", VE, None)
            p.dma("sync", SP_Y[T0 + w0:T0 + w0 + CH, :],
                  ysb[:, yb:yb + DI], SPL)

        per = (96 + nwq - 1) // nwq
        if t < nwq:
            emit_wq_batch(t * per, min((t + 1) * per, 96))
        if t == nwq - 1:
            p.wait("vector", TE, None)
            p.cp("vector", kcol[:, 1:4], wsum[:, 0:3], inc=VE)
            p.wait("tensor", VE, None)
            p.mm(p_a[0:1, 0:3], kcol[:, 0:1], kcol[:, 1:4], True, True,
                 inc=TE)
            p.wait("vector", TE, None)
            p.cp("vector", stat[0:1, 4:7], p_a[0:1, 0:3], inc=VE)
            p.wait("tensor", VE, None)
            p.mm(p_a[0:128, 4:7], onesf[0:1, 0:128], stat[0:1, 4:7],
                 True, True, inc=TE)
            p.wait("vector", TE, None)
            p.cp("vector", wsum[:, 5:8], p_a[0:128, 4:7], inc=VE)
            pb1["a"] = (VE, p.cnt[VE])
            p.ts("vector", wmean[:, 0:3], wsum[:, 5:8], 1.0 / (FF * DM),
                 EPS, ALU.mult, ALU.max)
            p.recip("vector", swcol[:, 0:1], wmean[:, 0:1])
            p.recip("vector", swcol[:, 1:2], wmean[:, 1:2])
            p.recip("vector", swcol[:, 2:3], wmean[:, 2:3], inc=VE)
            ve_wbc = p.cnt[VE]

    par_end = (NTILE * NCH) % 2
    p.wait("vector", VE, None)
    p.cp("vector", hpack[:, 0:DI],
         rhsa[64:80, par_end * DI:(par_end + 1) * DI])
    p.cp("vector", hpack[:, DI:DI + 1], ppcol[:], inc=VE)
    p.wait("sync", VE, None)
    p.dma("sync", CC_IN, hpack[:], SPL)
    cc_ld = p.cnt[SPL]

    p.wait("sync", SPL, None)
    p.wait("sync", SPLZ[0], None)
    p.wait("sync", SPLZ[1], None)
    p.wait("sync", SS, None)
    p.wait("sync", TR, None)
    p.wait("sync", LD, None)

    p.wait("gpsimd", SPL, cc_ld)
    if sim_cc:
        for j in range(4):
            p.dma("gpsimd", CC_OUT[j * 16:(j + 1) * 16, :], CC_IN, GP)
        p.wait("gpsimd", GP, None)
        p.nopinc("gpsimd", GC)
    else:
        p.op("gpsimd", [], [], lambda e: e.collective_compute(
            "AllGather", ALU.bypass,
            replica_groups=[[0, 1, 2, 3], [4, 5, 6, 7]],
            ins=[CC_IN], outs=[CC_OUT]), inc=GC)
    gc_cc = p.cnt[GC]

    p.barrier(("sync", "vector", "scalar", "tensor"), BAR)
    p.flush()
    ph1.close()

    phx = ExitStack()
    p_c = phx.enter_context(nc.psum_tensor("p_c", [128, TT], F32))
    p_d = phx.enter_context(nc.psum_tensor("p_d", [128, TT], F32))
    PBANKS = (p_a, p_b, p_c, p_d)
    pb_marks = [None, None, None, None]


    ph2 = ExitStack()
    sb2 = lambda n, s, d: ph2.enter_context(nc.sbuf_tensor(n, s, d))
    woutb = sb2("woutb", [128, DIC * DM], BF16)
    yfm = sb2("yfm", [128, 2 * DIC * TT], BF16)
    xcr = sb2("xcr", [128, DIC * TT], BF16)
    zr = sb2("zr", [128, DIC * TT], BF16)
    ygate = sb2("ygate", [128, DIC * TT], BF16)
    resb = sb2("resb", [128, DMC * TT], BF16)
    rest4 = sb2("rest4", [128, 4 * DM], BF16)
    x1t = sb2("x1t", [128, 4 * DM], BF16)
    aqt = sb2("aqt", [128, 4 * DM], BF16)
    sqt = sb2("sqt", [128, DM], F32)
    gsh = sb2("gsh", [16, DI + 1], F32)

    p.wait("sync", GC, gc_cc)
    p.mset("vector", hin[:], 0.0)
    for j in range(3):
        p.wait("sync", VE, None)
        p.dma("sync", gsh[:], CC_OUT[j * 16:(j + 1) * 16, :], LD)
        p.wait("vector", LD)
        p.ts("vector", stat[0:16, 9:10], gsh[:, DI:DI + 1],
             cmb_sb[:, 3 + j:4 + j], None, ALU.mult)
        p.tt("vector", stat[0:16, 9:10], stat[0:16, 9:10],
             cmb_sb[:, 6 + j:7 + j], ALU.add)
        p.ts("vector", gsh[:, 0:DI], gsh[:, 0:DI], cmb_sb[:, j:j + 1],
             None, ALU.mult)
        p.stt("vector", hin[:], hin[:], stat[0:16, 9:10], gsh[:, 0:DI],
              ALU.mult, ALU.add)
        p.nopinc("vector", VE)
    p.ts("vector", hin[:], hin[:], 100.0, -100.0, ALU.min, ALU.max)
    p.cp("vector", hinb[:], hin[:], inc=VE)
    ve_hin = p.cnt[VE]

    p.dma("gpsimd", woutb[:].rearrange("p (c f) -> p c f", c=DIC), WOUTr, GW)
    gw_wo = p.cnt[GW]

    p.wait("gpsimd", VE, None)
    p.dma("gpsimd", xbf[:].rearrange("p (c t) -> p c t", c=DMC),
          XTr[:, :, 3:3 + TT], GW)

    def b2_loads(tl, ve_guard):
        T0l = tl * TT
        pb = (tl % 2) * DIC * TT
        if ve_guard:
            p.wait("sync", VE, ve_guard)
        p.dma("sync", xcr[:].rearrange("p (c t) -> p c t", c=DIC),
              XCr[:, :, T0l:T0l + TT], LD)
        p.dma("sync", zr[:].rearrange("p (c t) -> p c t", c=DIC),
              Zr_[:, :, T0l:T0l + TT], LD)
        for dc in range(DIC):
            p.dma("sync", yfm[:, pb + dc * TT:pb + (dc + 1) * TT],
                  SP_Y[T0l:T0l + TT, dc * 128:(dc + 1) * 128], TR,
                  transpose=True)

    b2_loads(0, 0)
    for t in range(n_tiles):
        T0 = t * TT
        pb = (t % 2) * DIC * TT
        p.wait("vector", LD)
        p.wait("vector", TR)
        p.wait("vector", GW)
        p.nopinc("vector", VE)
        ve_rel = p.cnt[VE]

        p.wait("tensor", VE, ve_rel)
        if t == 0:
            p.wait("tensor", VE, ve_hin)
        for dc in range(DIC):
            bi = dc % 4
            bank = PBANKS[bi]
            if pb_marks[bi] is not None:
                p.wait("tensor", *pb_marks[bi])
            p.mm(bank[:], hinb[:, dc * 128:(dc + 1) * 128],
                 cgbuf[:, T0:T0 + TT], True, True, inc=TE)
            p.wait("vector", TE, p.cnt[TE])
            sl = slice(dc * TT, (dc + 1) * TT)
            p.tt("vector", ygate[:, sl], yfm[:, pb + dc * TT:pb + (dc + 1) * TT],
                 bank[:], ALU.add, inc=VE)
            pb_marks[bi] = (VE, p.cnt[VE])
        p.tt("vector", ygate[:], ygate[:], zr[:], ALU.mult)
        for dc in range(DIC):
            sl = slice(dc * TT, (dc + 1) * TT)
            p.stt("vector", ygate[:, sl], xcr[:, sl],
                  dpar_sb[:, dc:dc + 1], ygate[:, sl], ALU.mult, ALU.add)
        p.nopinc("vector", VE)

        ve_gate = p.cnt[VE]
        p.wait("tensor", VE, ve_gate)
        if t == 0:
            p.wait("tensor", GW, None)
        for dmc in range(DMC):
            bi = dmc % 4
            bank = PBANKS[bi]
            if pb_marks[bi] is not None:
                p.wait("tensor", *pb_marks[bi])
            for dc in range(DIC):
                lw = woutb[:, dc * 1024 + dmc * 128: dc * 1024 + (dmc + 1) * 128]
                p.mm(bank[:], lw, ygate[:, dc * TT:(dc + 1) * TT],
                     start=(dc == 0), stop=(dc == DIC - 1),
                     inc=(TE if dc == DIC - 1 else None))
            p.wait("vector", TE, p.cnt[TE])
            p.tt("vector", resb[:, dmc * TT:(dmc + 1) * TT], bank[:],
                 xbf[:, dmc * TT:(dmc + 1) * TT], ALU.add, inc=VE)
            pb_marks[bi] = (VE, p.cnt[VE])
        ve_res = p.cnt[VE]
        p.wait("sync", VE, None)
        p.dma("sync", RESr[:, :, T0:T0 + TT],
              resb[:].rearrange("p (c t) -> p c t", c=DMC), SPL)
        spl_res = p.cnt[SPL]
        if t + 1 < n_tiles:
            b2_loads(t + 1, ve_gate)

        p.wait("sync", SPL, spl_res)
        p.wait("sync", VE, None)
        for w in range(4):
            p.dma("sync", rest4[:, w * DM:(w + 1) * DM],
                  SP_RES[:, T0 + w * 128:T0 + w * 128 + 128],
                  TR, transpose=True)
        p.wait("vector", TR)
        for w in range(4):
            p.red("vector", lnst[:, w:w + 1], rest4[:, w * DM:(w + 1) * DM],
                  ALU.add)
        p.nopinc("vector", VE)
        p.wait("scalar", VE, p.cnt[VE])
        for w in range(4):
            p.act("scalar", sqt[:], rest4[:, w * DM:(w + 1) * DM], ACT.Square,
                  accum=lnst[:, 4 + w:5 + w], inc=AC)
        p.wait("vector", AC, p.cnt[AC])
        p.ts("vector", lnst[:, 8:12], lnst[:, 0:4], 1.0 / DM, None, ALU.mult)
        p.ts("vector", lnst[:, 12:16], lnst[:, 4:8], 1.0 / DM, None, ALU.mult)
        p.tt("vector", lnst[:, 16:20], lnst[:, 8:12], lnst[:, 8:12], ALU.mult)
        p.tt("vector", lnst[:, 20:24], lnst[:, 12:16], lnst[:, 16:20],
             ALU.subtract, inc=VE)
        p.wait("scalar", VE, p.cnt[VE])
        p.act("scalar", lnst[:, 24:28], lnst[:, 20:24], ACT.Ln,
              bias=epscol[:, 0:1])
        p.act("scalar", lnst[:, 28:32], lnst[:, 24:28], ACT.Exp,
              scale=nhalf[:, 0:1], inc=AC)
        p.wait("vector", AC, p.cnt[AC])
        p.wait("vector", SPL, None)
        for w in range(4):
            p.ts("vector", x1t[:, w * DM:(w + 1) * DM],
                 rest4[:, w * DM:(w + 1) * DM], lnst[:, 8 + w:9 + w],
                 lnst[:, 28 + w:29 + w], ALU.subtract, ALU.mult)
        for w in range(4):
            p.red("vector", lnst[:, 32 + w:33 + w],
                  x1t[:, w * DM:(w + 1) * DM], ALU.max, absval=True)
        p.ts("vector", lnst[:, 36:40], lnst[:, 32:36], EPS, None, ALU.max)
        p.recip("vector", lnst[:, 40:44], lnst[:, 36:40])
        p.ts("vector", lnst[:, 44:48], lnst[:, 40:44], 127.0, None, ALU.mult)
        p.ts("vector", invsa[:, t * 4:t * 4 + 4], lnst[:, 36:40],
             1.0 / 127.0, None, ALU.mult)
        p.nopinc("vector", VE)
        ve_ln = p.cnt[VE]
        p.wait("scalar", VE, ve_ln)
        p.wait("scalar", SPL, None)
        for w in range(4):
            p.act("scalar", sqt[:], x1t[:, w * DM:(w + 1) * DM], ACT.Identity,
                  bias=cmag[:, 0:1], scale=lnst[:, 44 + w:45 + w])
            p.act("scalar", aqt[:, w * DM:(w + 1) * DM], sqt[:], ACT.Identity,
                  bias=cmagn[:, 0:1], inc=(AC if w == 3 else None))
        p.wait("sync", VE, None)
        p.wait("sync", AC, None)
        p.dma("sync", SP_X1[T0:T0 + TT, :].rearrange("(w p) f -> p w f", p=128),
              x1t[:].rearrange("p (w f) -> p w f", w=4), SPL)
        p.dma("sync", SP_AQ[T0:T0 + TT, :].rearrange("(w p) f -> p w f", p=128),
              aqt[:].rearrange("p (w f) -> p w f", w=4), SPL)

        if t + 1 < n_tiles:
            p.wait("gpsimd", VE, ve_res)
            p.dma("gpsimd", xbf[:].rearrange("p (c t) -> p c t", c=DMC),
                  XTr[:, :, 3 + T0 + TT:3 + T0 + 2 * TT], GW)
        b1q = 48 if n_tiles == NTILE else 0
        pg = (96 - b1q + n_tiles - 1) // n_tiles
        emit_quant_tile(all_chunks[b1q + pg * t:
                                   min(b1q + pg * (t + 1), 96)], b1q + pg * t)

    p.barrier(("sync", "vector", "scalar", "tensor"), BAR)
    p.flush()
    ph2.close()
    ph12.close()
    phxs.close()

    ph3 = ExitStack()
    sb3 = lambda n, s, d: ph3.enter_context(nc.sbuf_tensor(n, s, d))
    wgub = sb3("wgub", [128, DMC * 2 * FF], BF16)
    aqf = sb3("aqf", [128, 2 * DMC * TT], BF16)
    rgb = sb3("rgb", [128, FF], BF16)
    ub = sb3("ub", [128, FF], BF16)
    hb = sb3("hb", [128, 2 * FF], BF16)
    hqp = sb3("hqp", [128, TT], F32)

    p.wait("sync", GP, None)
    p.wait("sync", SPL, None)
    ve_rgb_free = 0
    ve_ub_free = 0
    p.wait("sync", TR, None)
    p.wait("sync", LD, None)
    p.wait("sync", GW, None)
    for c in range(DMC):
        p.dma("sync", wgub[:, c * 2 * FF:(c + 1) * 2 * FF], QGUr[c], LD)
    ld_gu = p.cnt[LD]
    p.wait("tensor", LD, ld_gu)

    p.tt("vector", lnst[:, 40:41], wmean[:, 0:1], wmean[:, 0:1], ALU.mult)
    p.tt("vector", lnst[:, 40:41], lnst[:, 40:41], wmean[:, 1:2], ALU.mult)
    nsb = 4 * n_tiles
    p.tt("vector", lnst[:, 0:nsb], invsa[:, 0:nsb], invsa[:, 0:nsb], ALU.mult)
    p.tt("vector", lnst[:, 0:nsb], lnst[:, 0:nsb], invsa[:, 0:nsb], ALU.mult)
    p.ts("vector", lnst[:, 0:nsb], lnst[:, 0:nsb], lnst[:, 40:41], None,
         ALU.mult)

    pend = None

    def q2_piece(grp):
        pT0, ps0, ppar, pve = pend
        sl = slice(ppar * FF + grp * TT, ppar * FF + (grp + 1) * TT)
        if grp == 0:
            p.wait("scalar", VE, pve)
        p.act("scalar", hqp[:], hb[:, sl], ACT.Identity, bias=cmag[:, 0:1],
              scale=kcol[:, 6 + ppar:7 + ppar])
        p.act("scalar", hb[:, sl], hqp[:], ACT.Identity, bias=cmagn[:, 0:1],
              inc=(AC if grp == 7 else None))
        if grp == 7:
            p.wait("sync", AC, p.cnt[AC])
            p.dma("sync", SP_AQ2[pT0 + ps0:pT0 + ps0 + 128, :],
                  hb[:, ppar * FF:(ppar + 1) * FF], SPL)

    def b3_aq(tl):
        T0l = tl * TT
        ab = (tl % 2) * DMC * TT
        for dm in range(DMC):
            p.dma("sync", aqf[:, ab + dm * TT:ab + (dm + 1) * TT],
                  SP_AQ[T0l:T0l + TT, dm * 128:(dm + 1) * 128], TR,
                  transpose=True)

    sidx = 0
    te_end = [0, 0]
    b3_aq(0)
    for t in range(n_tiles):
        T0 = t * TT
        ab = (t % 2) * DMC * TT
        p.wait("tensor", TR, None)
        for sub in range(4):
            s0 = sub * 128
            par = sidx % 2
            hbs = hb[:, par * FF:(par + 1) * FF]
            for grp in range(8):
                bis = ((grp % 2) * 2, (grp % 2) * 2 + 1)
                for n in range(2):
                    if pb_marks[bis[n]] is not None:
                        p.wait("tensor", *pb_marks[bis[n]])
                for dm in range(DMC):
                    lw = aqf[:, ab + dm * TT + s0:ab + dm * TT + s0 + 128]
                    for n in range(2):
                        f = grp * 2 + n
                        bank = PBANKS[bis[n]]
                        p.mm(bank[:], lw,
                             wgub[:, dm * 2 * FF + f * TT:
                                  dm * 2 * FF + (f + 1) * TT],
                             start=(dm == 0), stop=(dm == DMC - 1),
                             inc=(TE if (dm == DMC - 1 and n == 1) else None))
                te_g = p.cnt[TE]
                for n in range(2):
                    f = grp * 2 + n
                    bank = PBANKS[bis[n]]
                    if f < 8:
                        p.wait("scalar", TE, te_g)
                        p.wait("scalar", VE, ve_rgb_free)
                        p.act("scalar", rgb[:, f * TT:(f + 1) * TT], bank[:],
                              ACT.Relu, inc=AC)
                        pb_marks[bis[n]] = (AC, p.cnt[AC])
                    else:
                        p.wait("vector", TE, te_g)
                        p.wait("vector", VE, ve_ub_free)
                        p.cp("vector", ub[:, (f - 8) * TT:(f - 7) * TT],
                             bank[:], inc=VE)
                        pb_marks[bis[n]] = (VE, p.cnt[VE])
                if pend is not None:
                    q2_piece(grp)
            p.wait("vector", AC, None)
            p.wait("vector", SPL, None)
            p.tt("vector", hbs, rgb[:], rgb[:], ALU.mult, inc=VE)
            ve_rgb_free = p.cnt[VE]
            p.tt("vector", hbs, hbs, ub[:], ALU.mult, inc=VE)
            ve_ub_free = p.cnt[VE]
            p.red("vector", kcol[:, 0:1], hbs, ALU.max, absval=True)
            k1 = lnst[:, t * 4 + sub:t * 4 + sub + 1]
            p.tt("vector", kcol[:, 3:4], kcol[:, 0:1], k1, ALU.mult)
            p.ts("vector", kcol[:, 3:4], kcol[:, 3:4], EPS, None, ALU.max)
            p.recip("vector", kcol[:, 4:5], kcol[:, 3:4])
            p.tt("vector", kcol[:, 5:6], kcol[:, 4:5], k1, ALU.mult)
            p.ts("vector", kcol[:, 6 + par:7 + par], kcol[:, 5:6], 127.0,
                 None, ALU.mult, inc=VE)
            ve_k = p.cnt[VE]
            p.ts("vector", invs2[:, t * 4 + sub:t * 4 + sub + 1], kcol[:, 3:4],
                 1.0 / 127.0, None, ALU.mult, inc=VE)
            pend = (T0, s0, par, ve_k)
            sidx += 1
        if t + 1 < n_tiles:
            p.wait("sync", TE, te_end[(t + 1) % 2])
            b3_aq(t + 1)
        te_end[t % 2] = p.cnt[TE]

    for grp in range(8):
        q2_piece(grp)
    pend = None

    p.barrier(("sync", "vector", "scalar", "tensor"), BAR)
    p.flush()
    ph3.close()

    ph4 = ExitStack()
    sb4 = lambda n, s, d: ph4.enter_context(nc.sbuf_tensor(n, s, d))
    wdb = sb4("wdb", [128, FFC * DM], BF16)
    aq2f = sb4("aq2f", [128, 2 * FFC * TT], BF16)
    x1r = sb4("x1r", [128, 2 * 4 * DM], BF16)
    dnb4 = sb4("dnb4", [128, 4 * DM], F32)
    ost = sb4("ost", [128, 4 * DM], F32)

    p.wait("sync", SPL, None)
    p.wait("sync", TR, None)
    p.wait("sync", LD, None)
    p.wait("sync", GP, None)
    for c in range(FFC):
        p.dma("sync", wdb[:, c * DM:(c + 1) * DM], QDr[c], LD)
    ld_d = p.cnt[LD]
    p.wait("tensor", LD, ld_d)

    def b4_loads(tl, ve_guard, te_guard):
        T0l = tl * TT
        ab = (tl % 2) * FFC * TT
        xb = (tl % 2) * 4 * DM
        if ve_guard:
            p.wait("sync", VE, ve_guard)
        p.dma("sync", x1r[:, xb:xb + 4 * DM].rearrange("p (w f) -> p w f", w=4),
              SP_X1[T0l:T0l + TT, :].rearrange("(w p) f -> p w f", p=128), LD)
        if te_guard:
            p.wait("sync", TE, te_guard)
        for ic in range(FFC):
            p.dma("sync", aq2f[:, ab + ic * TT:ab + (ic + 1) * TT],
                  SP_AQ2[T0l:T0l + TT, ic * 128:(ic + 1) * 128],
                  TR, transpose=True)

    te_end = [0, 0]
    ve_x1 = [0, 0]
    b4_loads(0, 0, 0)
    for t in range(n_tiles):
        T0 = t * TT
        ab = (t % 2) * FFC * TT
        xb = (t % 2) * 4 * DM
        p.wait("tensor", TR, None)
        for sub in range(4):
            s0 = sub * 128
            bis = ((sub % 2) * 2, (sub % 2) * 2 + 1)
            for n in range(2):
                if pb_marks[bis[n]] is not None:
                    p.wait("tensor", *pb_marks[bis[n]])
            for ic in range(FFC):
                lw = aq2f[:, ab + ic * TT + s0:ab + ic * TT + s0 + 128]
                for n in range(2):
                    bank = PBANKS[bis[n]]
                    p.mm(bank[:], lw,
                         wdb[:, ic * DM + n * TT:ic * DM + (n + 1) * TT],
                         start=(ic == 0), stop=(ic == FFC - 1),
                         inc=(TE if (ic == FFC - 1 and n == 1) else None))
            te_d = p.cnt[TE]
            p.wait("vector", TE, te_d)
            if sub == 0:
                p.wait("vector", LD, None)
                p.wait("vector", STO, None)
            iv = invs2[:, t * 4 + sub:t * 4 + sub + 1]
            wsl = slice(sub * DM, (sub + 1) * DM)
            for n in range(2):
                bank = PBANKS[bis[n]]
                p.ts("vector", dnb4[:, sub * DM + n * TT:
                                sub * DM + (n + 1) * TT], bank[:], iv,
                     wmean[:, 2:3], ALU.mult, ALU.mult, inc=VE)
                pb_marks[bis[n]] = (VE, p.cnt[VE])
            p.tt("vector", dnb4[:, wsl], dnb4[:, wsl], x1r[:, xb + wsl.start:
                 xb + wsl.stop], ALU.add,
                 inc=(VE if sub == 3 else None))
        ve_x1[t % 2] = p.cnt[VE]
        if t + 1 < n_tiles:
            b4_loads(t + 1, ve_x1[(t + 1) % 2], te_end[(t + 1) % 2])
        te_end[t % 2] = p.cnt[TE]

        for w in range(4):
            p.red("vector", lnst[:, w:w + 1], dnb4[:, w * DM:(w + 1) * DM],
                  ALU.add)
        p.nopinc("vector", VE)
        p.wait("scalar", VE, p.cnt[VE])
        p.wait("scalar", STO, None)
        for w in range(4):
            p.act("scalar", ost[:, w * DM:(w + 1) * DM],
                  dnb4[:, w * DM:(w + 1) * DM], ACT.Square,
                  accum=lnst[:, 4 + w:5 + w], inc=AC)
        p.wait("vector", AC, p.cnt[AC])
        p.ts("vector", lnst[:, 8:12], lnst[:, 0:4], 1.0 / DM, None, ALU.mult)
        p.ts("vector", lnst[:, 12:16], lnst[:, 4:8], 1.0 / DM, None, ALU.mult)
        p.tt("vector", lnst[:, 16:20], lnst[:, 8:12], lnst[:, 8:12], ALU.mult)
        p.tt("vector", lnst[:, 20:24], lnst[:, 12:16], lnst[:, 16:20],
             ALU.subtract, inc=VE)
        p.wait("scalar", VE, p.cnt[VE])
        p.act("scalar", lnst[:, 24:28], lnst[:, 20:24], ACT.Ln,
              bias=epscol[:, 0:1])
        p.act("scalar", lnst[:, 28:32], lnst[:, 24:28], ACT.Exp,
              scale=nhalf[:, 0:1], inc=AC)
        p.wait("vector", AC, p.cnt[AC])
        for w in range(4):
            p.ts("vector", ost[:, w * DM:(w + 1) * DM],
                 dnb4[:, w * DM:(w + 1) * DM], lnst[:, 8 + w:9 + w],
                 lnst[:, 28 + w:29 + w], ALU.subtract, ALU.mult)
        p.nopinc("vector", VE)
        p.wait("gpsimd", VE, None)
        for w in range(4):
            p.dma("gpsimd", OUT[T0 + w * 128:T0 + w * 128 + 128, :],
                  ost[:, w * DM:(w + 1) * DM], STO)

    p.wait("sync", STO, None)
    p.flush()
    ph4.close()
    phx.close()
    ctx.close()
    return nc


_CACHE = {}


def kernel(L0_attn_x, L0_attn_W_in, L0_attn_conv_w, L0_attn_conv_b,
           L0_attn_W_xproj, L0_attn_A_log, L0_attn_D_param, L0_attn_W_out,
           L0_ffn_w_gate, L0_ffn_w_up, L0_ffn_w_down):
    x = np.asarray(L0_attn_x, np.float32)
    f32 = lambda a: np.ascontiguousarray(np.asarray(a, np.float32))

    xpad = np.pad(x, ((0, 0), (3, 0), (0, 0)))
    xt = np.ascontiguousarray(xpad.transpose(0, 2, 1))

    w_inT = f32(np.asarray(L0_attn_W_in).T)
    wxT = f32(np.asarray(L0_attn_W_xproj).T)
    w_outT = f32(np.asarray(L0_attn_W_out).T)
    wguT = f32(np.concatenate([np.asarray(L0_ffn_w_gate).T,
                               np.asarray(L0_ffn_w_up).T], axis=1))
    wdT = f32(np.asarray(L0_ffn_w_down).T)
    convw = f32(np.asarray(L0_attn_conv_w).reshape(16, 128, 4)
                .transpose(1, 0, 2).reshape(128, 64))
    convb = f32(np.asarray(L0_attn_conv_b).reshape(16, 128).T)
    dparam = f32(np.asarray(L0_attn_D_param).reshape(16, 128).T)
    alog48 = f32(np.tile(np.asarray(L0_attn_A_log), 3)[:, None])

    if "nc" not in _CACHE:
        _CACHE["nc"] = build_nc()
    nc = _CACHE["nc"]
    in_maps = _stage(x, xt, w_inT, wxT, w_outT, wguT, wdT, convw, convb,
                     dparam, alog48)
    res = run_bass_kernel_spmd(nc, in_maps, core_ids=list(range(8)))
    out = np.empty((B, L, DM), np.float32)
    for k in range(8):
        b, q = k // 4, k % 4
        out[b, q * TOK:(q + 1) * TOK] = np.asarray(res.results[k]["out"])
    return out


def _stage(x, xt, w_inT, wxT, w_outT, wguT, wdT, convw, convb, dparam,
           alog48):
    mask = np.zeros((CH, AUGK), np.float32)
    for tau in range(CH):
        mask[tau, tau:CH] = 536870912.0
    mask[:, CH:] = 536870912.0
    ident16 = np.eye(16, dtype=np.float32)
    in_maps = []
    for k in range(8):
        b, q = k // 4, k % 4
        mh = [1.0 if j < q else 0.0 for j in range(3)]
        mp = [1.0 if (0 < j < q) else 0.0 for j in range(3)]
        cmb = np.tile(np.array(mh + mp + [1.0 - v for v in mp], np.float32),
                      (16, 1))
        in_maps.append({
            "xt": np.ascontiguousarray(xt[b][:, q * TOK: q * TOK + TOK + 3]),
            "w_inT": w_inT, "wxT": wxT, "convw": convw, "convb": convb,
            "dparam": dparam, "alog48": alog48, "w_outT": w_outT,
            "wguT": wguT, "wdT": wdT, "mask": mask, "ident16": ident16,
            "cmb": np.ascontiguousarray(cmb),
        })
    return in_maps


# version 14
# speedup vs baseline: 1.0001x; 1.0001x over previous
import numpy as np
from contextlib import ExitStack
import concourse.bass as bass
import concourse.mybir as mybir
from concourse.bass_utils import run_bass_kernel_spmd

F32 = mybir.dt.float32
BF16 = mybir.dt.bfloat16
AX = mybir.AxisListType.X
ALU = mybir.AluOpType
ACT = mybir.ActivationFunctionType

B, L, DM, DI, DS, FF = 2, 16384, 1024, 2048, 16, 4096
TOK, TT = 4096, 512
NTILE = TOK // TT
CH = 64
NCH = TT // CH
AUGK = CH + DS
DMC, DIC, FFC = DM // 128, DI // 128, FF // 128
C23 = 12582912.0
C52 = 6755399441055744.0
PROBE = 1.5 * 2.0 ** -30
EPS = 1e-5


class Prog:
    ENGINES = ("sync", "vector", "scalar", "tensor", "gpsimd")

    def __init__(self, nc, ctx):
        self.nc, self.ctx = nc, ctx
        self.q = {e: [] for e in self.ENGINES}
        self.sems, self.cnt = {}, {}
        self.w = {e: [] for e in self.ENGINES}
        self.r = {e: [] for e in self.ENGINES}
        self.maxwait = {}
        self.synced = {}

    def sem(self, name):
        if name not in self.sems:
            self.sems[name] = self.ctx.enter_context(self.nc.semaphore(name))
            self.cnt[name] = 0
        return self.sems[name]

    def emit(self, eng, fn):
        self.q[eng].append(fn)

    @staticmethod
    def _foot(ap):
        try:
            t = ap.tensor
            name = t.name
            w = int(t.shape[-1])
            off = int(ap.offset)
            p0, c0 = off // w, off % w
            pext = 0
            cext = 0
            for s, n in ap.ap:
                if n <= 1:
                    continue
                if s > 0 and s % w == 0:
                    pext += (n - 1) * (s // w)
                else:
                    cext += (n - 1) * s
            return (name, p0, p0 + pext, c0, c0 + cext)
        except Exception:
            try:
                return (ap.name, 0, 1 << 30, 0, 1 << 30)
            except Exception:
                return None

    @staticmethod
    def _ovl(a, b):
        return (a[0] == b[0] and a[1] <= b[2] and b[1] <= a[2]
                and a[3] <= b[4] and b[3] <= a[4])

    def _predrain(self, eng, outs, ins):
        ni = [f for f in (self._foot(a) for a in ins) if f]
        no = [f for f in (self._foot(a) for a in outs) if f]
        hit = any(self._ovl(i, w) for i in ni for w in self.w[eng]) or \
            any(self._ovl(o, x) for o in no
                for x in (self.w[eng] + self.r[eng]))
        if hit or len(self.w[eng]) + len(self.r[eng]) > 512:
            self.emit(eng, lambda e: e.drain())
            self.w[eng], self.r[eng] = [], []
        self.w[eng] += no
        self.r[eng] += ni

    def _fin(self, eng, fn, inc):
        if inc is None:
            self.emit(eng, fn)
        else:
            h = self.sem(inc)
            self.cnt[inc] += 1
            self.emit(eng, lambda e: fn(e).then_inc(h, 1))

    def op(self, eng, outs, ins, fn, inc=None):
        self._predrain(eng, outs, ins)
        self._fin(eng, fn, inc)

    def wait(self, eng, sem, val=None):
        h = self.sem(sem)
        v = self.cnt[sem] if val is None else val
        if v > 0:
            self.emit(eng, lambda e: e.wait_ge(h, v))
            self.maxwait[sem] = max(self.maxwait.get(sem, 0), v)
            k = (eng, sem)
            self.synced[k] = max(self.synced.get(k, 0), v)

    def dma(self, eng, out, in_, sem, transpose=False):
        h = self.sem(sem)
        mw = self.maxwait.get(sem, 0)
        if self.synced.get((eng, sem), 0) < mw:
            self.wait(eng, sem, mw)
        self.cnt[sem] += 16
        self.emit(eng, lambda e: e.dma_start(
            out=out, in_=in_, transpose=transpose).then_inc(h, 16))

    def ts(self, eng, out, in0, s1, s2, op0, op1=None, inc=None):
        ins = [in0] + [s for s in (s1, s2) if hasattr(s, "tensor")]
        if op1 is None:
            self.op(eng, [out], ins,
                    lambda e: e.tensor_scalar(out, in0, s1, None, op0), inc)
        else:
            self.op(eng, [out], ins,
                    lambda e: e.tensor_scalar(out, in0, s1, s2, op0, op1), inc)

    def tt(self, eng, out, a, b, alu, inc=None):
        self.op(eng, [out], [a, b], lambda e: e.tensor_tensor(out, a, b, alu), inc)

    def stt(self, eng, out, in0, s, in1, op0, op1, inc=None):
        ins = [in0, in1] + ([s] if hasattr(s, "tensor") else [])
        self.op(eng, [out], ins,
                lambda e: e.scalar_tensor_tensor(out, in0, s, in1, op0, op1), inc)

    def scan(self, eng, out, d0, d1, init, op0, op1, inc=None):
        ins = [d0, d1] + ([init] if hasattr(init, "tensor") else [])
        self.op(eng, [out], ins,
                lambda e: e.tensor_tensor_scan(out, d0, d1, init, op0, op1), inc)

    def red(self, eng, out, in_, alu, absval=False, inc=None, axis=AX):
        self.op(eng, [out], [in_],
                lambda e: e.tensor_reduce(out, in_, axis=axis, op=alu,
                                          apply_absolute_value=absval), inc)

    def cp(self, eng, out, in_, inc=None):
        self.op(eng, [out], [in_], lambda e: e.tensor_copy(out, in_), inc)

    def act(self, eng, out, in_, func, bias=0.0, scale=1.0, accum=None, inc=None):
        ins = [in_] + [a for a in (bias, scale) if hasattr(a, "tensor")]
        outs = [out] + ([accum] if accum is not None else [])
        if accum is None:
            self.op(eng, outs, ins, lambda e: e.activation(
                out, in_, func, bias=bias, scale=scale), inc)
        else:
            self.op(eng, outs, ins, lambda e: e.activation(
                out, in_, func, bias=bias, scale=scale, accum_out=accum), inc)

    def recip(self, eng, out, in_, inc=None):
        self.op(eng, [out], [in_], lambda e: e.reciprocal(out, in_), inc)

    def mset(self, eng, ap, val, inc=None):
        self.op(eng, [ap], [], lambda e: e.memset(ap, val), inc)

    def mm(self, out, lhsT, rhs, start, stop, tp=None, inc=None):
        self._fin("tensor", lambda e: e.matmul(out, lhsT, rhs, start=start,
                                               stop=stop, tile_position=tp), inc)

    def nopinc(self, eng, sem):
        h = self.sem(sem)
        self.cnt[sem] += 1
        if eng == "sync":
            self.emit(eng, lambda e: e.nop().then_inc(h, 1))
        else:
            self.emit(eng, lambda e: e.drain().then_inc(h, 1))
            self.w[eng], self.r[eng] = [], []

    def barrier(self, engines, sem):
        for e in engines:
            self.nopinc(e, sem)
        tgt = self.cnt[sem]
        for e in engines:
            self.wait(e, sem, tgt)

    def flush(self):
        with self.nc.Block() as block:
            for name in self.ENGINES:
                ops = self.q[name]
                if not ops:
                    continue
                def body(e, ops=ops):
                    for f in ops:
                        f(e)
                getattr(block, name)(body)
        self.q = {e: [] for e in self.ENGINES}
        self.w = {e: [] for e in self.ENGINES}
        self.r = {e: [] for e in self.ENGINES}


def build_nc(n_tiles=NTILE, sim_cc=False, dbg=False):
    nc = bass.Bass("TRN2", target_bir_lowering=False, debug=False)
    ctx = ExitStack()

    dram = lambda n, s, d, k="ExternalInput": nc.dram_tensor(n, s, d, kind=k).ap()
    XT = dram("xt", [DM, TOK + 3], F32)
    WIN = dram("w_inT", [DM, 2 * DI], F32)
    WX = dram("wxT", [DI, 33], F32)
    CONVW = dram("convw", [128, 4 * DIC], F32)
    CONVB = dram("convb", [128, DIC], F32)
    DPAR = dram("dparam", [128, DIC], F32)
    ALOG = dram("alog48", [48, 1], F32)
    WOUT = dram("w_outT", [DI, DM], F32)
    WGU = dram("wguT", [DM, 2 * FF], F32)
    WD = dram("wdT", [FF, DM], F32)
    MASK = dram("mask", [CH, AUGK], F32)
    I16 = dram("ident16", [16, 16], F32)
    CMB = dram("cmb", [16, 9], F32)
    OUT = dram("out", [TOK, DM], F32, "ExternalOutput")

    kindd = "ExternalOutput" if dbg else "Internal"
    SP_XC = dram("sp_xc", [DI, TOK], BF16, kindd)
    SP_Z = dram("sp_z", [DI, TOK], BF16, kindd)
    SP_Y = dram("sp_y", [TOK, DI], BF16, kindd)
    SP_RES = dram("sp_res", [DM, TOK], BF16, kindd)
    SP_X1 = dram("sp_x1", [TOK, DM], BF16, kindd)
    SP_AQ = dram("sp_aq", [TOK, DM], BF16, kindd)
    SP_AQ2 = dram("sp_aq2", [TOK, FF], BF16, kindd)
    SP_QGU = dram("sp_qgu", [DM, 2 * FF], BF16, kindd)
    SP_QD = dram("sp_qd", [FF, DM], BF16, kindd)
    CC_IN = dram("cc_in", [16, DI + 1], F32, "Internal")
    CC_OUT = dram("cc_out", [64, DI + 1], F32, "Internal")

    XTr = XT.rearrange("(c p) t -> p c t", p=128)
    WINr = WIN.rearrange("(c p) f -> c p f", p=128)
    WXr = WX.rearrange("(c p) f -> p c f", p=128)
    WOUTr = WOUT.rearrange("(c p) f -> p c f", p=128)
    WGUr = WGU.rearrange("(c p) f -> c p f", p=128)
    WDr = WD.rearrange("(c p) f -> c p f", p=128)
    QGUr = SP_QGU.rearrange("(c p) f -> c p f", p=128)
    QDr = SP_QD.rearrange("(c p) f -> c p f", p=128)
    XCr = SP_XC.rearrange("(c p) t -> p c t", p=128)
    Zr_ = SP_Z.rearrange("(c p) t -> p c t", p=128)
    RESr = SP_RES.rearrange("(c p) t -> p c t", p=128)

    sbuf = lambda n, s, d: ctx.enter_context(nc.sbuf_tensor(n, s, d))
    psum = lambda n, s: ctx.enter_context(nc.psum_tensor(n, s, F32))

    convw_sb = sbuf("convw_sb", [128, 4 * DIC], F32)
    convb_sb = sbuf("convb_sb", [128, DIC], F32)
    dpar_sb = sbuf("dpar_sb", [128, DIC], F32)
    acol = sbuf("acol", [48, 1], F32)
    onecol = sbuf("onecol", [65, 1], F32)
    onesf = sbuf("onesf", [65, 128], F32)
    cprobe = sbuf("cprobe", [128, 1], F32)
    cmag = sbuf("cmag", [128, 1], F32)
    cmagn = sbuf("cmagn", [128, 1], F32)
    epscol = sbuf("epscol", [128, 1], F32)
    nhalf = sbuf("nhalf", [128, 1], F32)
    maskb = sbuf("maskb", [CH, AUGK], F32)
    i16b = sbuf("i16b", [16, 16], BF16)
    cmb_sb = sbuf("cmb_sb", [16, 9], F32)
    wxb = sbuf("wxb", [128, DIC * 33], BF16)
    ppcol = sbuf("ppcol", [16, 1], F32)
    invsa = sbuf("invsa", [128, 32], F32)
    invs2 = sbuf("invs2", [128, 32], F32)
    wsum = sbuf("wsum", [128, 8], F32)
    wmean = sbuf("wmean", [128, 4], F32)
    swcol = sbuf("swcol", [128, 4], F32)
    xmb = sbuf("xmb", [128, 24], BF16)
    stat = sbuf("stat", [128, 16], F32)
    kcol = sbuf("kcol", [128, 8], F32)
    qcol = sbuf("qcol", [128, 3], F32)
    lnst = sbuf("lnst", [128, 48], F32)

    phxs = ExitStack()
    wstg = phxs.enter_context(nc.sbuf_tensor("wstg", [128, 2 * 1024], F32))
    wscr = phxs.enter_context(nc.sbuf_tensor("wscr", [128, 1024], F32))
    wq2 = phxs.enter_context(nc.sbuf_tensor("wq2", [128, 2 * 1024], BF16))

    ph12 = ExitStack()
    s12 = lambda n, s, d: ph12.enter_context(nc.sbuf_tensor(n, s, d))
    cgbuf = s12("cgbuf", [16, TOK], BF16)
    hin = s12("hin", [16, DI], F32)
    hinb = s12("hinb", [16, DI], BF16)
    xbf = s12("xbf", [128, DMC * TT], BF16)

    p_a = psum("p_a", [128, TT])
    p_b = psum("p_b", [128, TT])

    p = Prog(nc, ctx)
    LD, SPL, TR, SS, GP = "s_ld", "s_spl", "s_tr", "s_ss", "s_gp"
    TE, VE, AC, GC = "s_te", "s_ve", "s_ac", "s_gc"
    EVZV, EVZA = "s_evzv", "s_evza"
    BAR, STO, GW = "s_bar", "s_sto", "s_gw"
    SPLZ = ("s_splz0", "s_splz1")
    SSB = "s_ssb"
    for s in (LD, SPL, TR, SS, SSB, GP, TE, VE, AC, GC, EVZV, EVZA, BAR, STO,
              GW) + SPLZ:
        p.sem(s)

    ph1 = ExitStack()
    sb1 = lambda n, s, d: ph1.enter_context(nc.sbuf_tensor(n, s, d))
    ps1 = lambda n, s: ph1.enter_context(nc.psum_tensor(n, s, F32))
    hpack = sb1("hpack", [16, DI + 1], F32)
    p_x = ps1("p_x", [16, TT])
    p_dt2 = ps1("p_dt2", [16, TT])
    p_g = ps1("p_g", [128, TT])
    p_h = ps1("p_h", [128, TT])
    p_m1 = ps1("p_m1", [CH, AUGK])
    p_m2 = ps1("p_m2", [AUGK, AUGK])
    winb = sb1("winb", [128, DMC * 2 * DI], BF16)
    xproj = sb1("xproj", [128, DIC * (TT + 3)], BF16)
    xconv = sb1("xconv", [128, DIC * TT], BF16)
    zsilc = sb1("zsilc", [128, 2 * TT], BF16)
    uw = sb1("uw", [128, 2 * DI], BF16)
    rhsa = sb1("rhsa", [AUGK, 2 * DI], BF16)
    ysb = sb1("ysb", [CH, 2 * DI], BF16)
    dta = sb1("dta", [16, TT], F32)
    s16 = sb1("s16", [16, TT], F32)
    dtsb = sb1("dtsb", [16, TT], F32)
    braw = sb1("braw", [16, TT], F32)
    bbar = sb1("bbar", [16, TT], F32)
    cevac = sb1("cevac", [16, TT], F32)
    fbc = sb1("fbc", [16, TT], F32)
    eargB = sb1("eargB", [16, TT], F32)
    eargC = sb1("eargC", [16, TT], F32)
    bpp = sb1("bpp", [16, TT], BF16)
    cpk = sb1("cpk", [16, TT], BF16)
    ccb = sb1("ccb", [16, 24], F32)
    ecb = sb1("ecb", [16, 24], F32)
    sp8 = sb1("sp8", [16, 8], F32)
    scn8 = sb1("scn8", [16, 24], F32)
    diag2a = sb1("diag2a", [16, 8 * 16], BF16)
    rhsm = sb1("rhsm", [16, NCH * AUGK], BF16)
    taug = sb1("taug", [AUGK, 2 * AUGK], BF16)
    dtch = sb1("dtch", [1, TT], F32)

    p.dma("sync", convw_sb[:], CONVW, LD)
    p.dma("sync", convb_sb[:], CONVB, LD)
    p.dma("sync", dpar_sb[:], DPAR, LD)
    p.dma("sync", acol[:], ALOG, LD)
    p.dma("sync", maskb[:], MASK, LD)
    p.dma("sync", cmb_sb[:], CMB, LD)
    p.dma("sync", hpack[0:16, 0:16], I16, LD)
    for c in range(DMC):
        p.dma("gpsimd", winb[:, c * 4096:(c + 1) * 4096], WINr[c], GW)
    p.dma("gpsimd", wxb[:].rearrange("p (c f) -> p c f", c=DIC), WXr, GW)
    p.dma("gpsimd", xmb[:].rearrange("p (c t) -> p c t", c=8),
          XTr[:, :, 0:3], GW)
    gw_w = p.cnt[GW]

    p.wait("vector", LD)
    p.mset("vector", cprobe[:], PROBE)
    p.ts("vector", cprobe[:], cprobe[:], C23, -C23, ALU.add, ALU.add)
    p.ts("vector", cprobe[:], cprobe[:], 536870912.0, None, ALU.mult)
    p.ts("vector", cmag[:], cprobe[:], -C23, C23, ALU.mult, ALU.add)
    p.stt("vector", cmag[:], cprobe[:], C52, cmag[:], ALU.mult, ALU.add)
    p.ts("vector", cmagn[:], cmag[:], -1.0, None, ALU.mult)
    p.mset("vector", epscol[:], EPS)
    p.mset("vector", qcol[:, 0:1], 1.0)
    p.mset("vector", qcol[:, 1:2], 2.0)
    p.mset("vector", qcol[:, 2:3], -1.0)
    p.mset("vector", nhalf[:], -0.5)
    p.mset("vector", onecol[:], 1.0)
    p.mset("vector", onesf[:], 1.0)
    p.mset("vector", ppcol[:], 1.0)
    p.mset("vector", kcol[:, 0:1], 1.0)
    p.cp("vector", i16b[:], hpack[0:16, 0:16])
    p.mset("vector", hpack[:], 0.0)
    p.mset("vector", rhsa[64:80, 0:DI], 0.0)
    p.nopinc("vector", VE)
    ve_consts = p.cnt[VE]
    p.wait("scalar", LD, None)
    p.act("scalar", acol[:], acol[:], ACT.Exp, inc=AC)
    p.wait("vector", AC, p.cnt[AC])
    p.ts("vector", acol[:], acol[:], -1.0, None, ALU.mult, inc=VE)
    ve_acol = p.cnt[VE]

    wq_chunks = []
    for c in range(DMC):
        for j in range(8):
            col = j * 1024
            wq_chunks.append((WGUr[c][:, col:col + 1024], 1024, j // 4))
    for c in range(FFC):
        wq_chunks.append((WDr[c], 1024, 2))
    wq_marks = {}
    wq_first = [True, True, True]

    def emit_wq_batch(lo, hi):
        for k in range(lo, hi):
            src_ap, width, tgt = wq_chunks[k]
            par = k % 2
            gsl = wstg[:, par * 1024: par * 1024 + width]
            if k >= 2:
                p.wait("gpsimd", VE, wq_marks[k - 2])
            else:
                p.wait("gpsimd", VE, ve_consts)
            p.dma("gpsimd", gsl, src_ap, GP)
            p.wait("vector", GP, None)
            p.red("vector", wsum[:, 4:5], gsl, ALU.add, absval=True)
            if wq_first[tgt]:
                p.cp("vector", wsum[:, tgt:tgt + 1], wsum[:, 4:5], inc=VE)
                wq_first[tgt] = False
            else:
                p.tt("vector", wsum[:, tgt:tgt + 1], wsum[:, tgt:tgt + 1],
                     wsum[:, 4:5], ALU.add, inc=VE)
            wq_marks[k] = p.cnt[VE]

    gu_chunks = []
    for c in range(DMC):
        for j in range(8):
            col = j * 1024
            gu_chunks.append((WGUr[c][:, col:col + 1024],
                              QGUr[c][:, col:col + 1024], j // 4))
    wd_chunks = [(WDr[c], QDr[c], 2) for c in range(FFC)]
    all_chunks = gu_chunks + wd_chunks
    qac_marks = {}

    def _quant_load(ch, base, i):
        k = base + i
        if k - 2 in qac_marks:
            p.wait("gpsimd", AC, qac_marks[k - 2])
        p.dma("gpsimd", wstg[:, (k % 2) * 1024:(k % 2) * 1024 + 1024],
              ch[i][0], GP)

    def emit_quant_tile(ch, base):
        for i in range(min(2, len(ch))):
            _quant_load(ch, base, i)
        for i in range(len(ch)):
            k = base + i
            par = k % 2
            sl = wstg[:, par * 1024:par * 1024 + 1024]
            ql = wq2[:, par * 1024:par * 1024 + 1024]
            p.wait("scalar", GP)
            p.act("scalar", wscr[:], sl, ACT.Identity, bias=cmag[:, 0:1],
                  scale=swcol[:, ch[i][2]:ch[i][2] + 1])
            p.act("scalar", sl, wscr[:], ACT.Identity, bias=cmagn[:, 0:1])
            p.act("scalar", wscr[:], sl, ACT.Relu, bias=qcol[:, 0:1])
            p.act("scalar", sl, wscr[:], ACT.Relu, bias=qcol[:, 1:2],
                  scale=qcol[:, 2:3])
            p.act("scalar", ql, sl, ACT.Identity,
                  bias=qcol[:, 0:1], scale=qcol[:, 2:3], inc=AC)
            qac_marks[k] = p.cnt[AC]
            p.wait("gpsimd", AC, qac_marks[k])
            p.dma("gpsimd", ch[i][1], ql, GP)
            if i + 2 < len(ch):
                _quant_load(ch, base, i + 2)

    ve_wbc = 0
    nwq = min(2, n_tiles)
    pb1 = {"a": None, "b": None, "g": None, "h": None}
    te_aug_end = [0]

    for t in range(n_tiles):
        T0 = t * TT
        if t > 0:
            p.wait("gpsimd", TE, te_xz)
        p.dma("gpsimd", xbf[:].rearrange("p (c t) -> p c t", c=DMC),
              XTr[:, :, 3 + T0:3 + T0 + TT], GW)
        if t > 0:
            p.op("vector", [xproj[:]], [xproj[:]], lambda e: e.tensor_copy(
                xproj[:].rearrange("p (c m) -> p c m", c=DIC)[:, :, 0:3],
                xproj[:].rearrange("p (c m) -> p c m", c=DIC)[:, :, TT:TT + 3]))
            p.nopinc("vector", VE)
        ve_x = p.cnt[VE]

        p.wait("tensor", VE, ve_x)
        p.wait("tensor", GW, None)
        ac_xp = {}
        for fc in range(32):
            bk = ("a", "b")[fc % 2]
            bank = (p_a, p_b)[fc % 2]
            if pb1[bk] is not None:
                p.wait("tensor", *pb1[bk])
            for dm in range(DMC):
                lw = winb[:, dm * 4096 + fc * 128: dm * 4096 + (fc + 1) * 128]
                p.mm(bank[:], lw, xbf[:, dm * TT:(dm + 1) * TT],
                     start=(dm == 0), stop=(dm == DMC - 1),
                     inc=(TE if dm == DMC - 1 else None))
            te_fc = p.cnt[TE]
            te_xz = te_fc
            if fc < 16:
                if fc == 0:
                    p.wait("scalar", VE, ve_x)
                p.wait("scalar", TE, te_fc)
                p.act("scalar", xproj[:, fc * (TT + 3) + 3:
                                      (fc + 1) * (TT + 3)],
                      bank[:], ACT.Identity, inc=EVZA)
                pb1[bk] = (EVZA, p.cnt[EVZA])
                ac_xp[fc] = (EVZA, p.cnt[EVZA])
            else:
                c = fc - 16
                par = c % 2
                p.wait("scalar", SPLZ[par], None)
                p.wait("scalar", TE, te_fc)
                p.act("scalar", zsilc[:, par * TT:(par + 1) * TT], bank[:],
                      ACT.Silu, inc=EVZA)
                pb1[bk] = (EVZA, p.cnt[EVZA])
                p.wait("sync", EVZA, None)
                p.dma("sync", Zr_[:, c:c + 1, T0:T0 + TT],
                      zsilc[:, par * TT:(par + 1) * TT]
                      .rearrange("p (c t) -> p c t", c=1), SPLZ[par])

        if t == 0:
            for bb in ("a", "b"):
                if pb1[bb] is not None:
                    p.wait("tensor", *pb1[bb])
            for fc in range(16):
                for dm in range(DMC):
                    lw = winb[:, dm * 4096 + fc * 128: dm * 4096 + (fc + 1) * 128]
                    p.mm(p_a[:, fc * 3:(fc + 1) * 3], lw,
                         xmb[:, dm * 3:(dm + 1) * 3],
                         start=(fc == 0 and dm == 0),
                         stop=(fc == 15 and dm == DMC - 1))
            p.nopinc("tensor", TE)
            p.wait("vector", TE, p.cnt[TE])
            p.op("vector", [xproj], [p_a], lambda e: e.tensor_copy(
                xproj[:].rearrange("p (c m) -> p c m", c=DIC)[:, :, 0:3],
                p_a[:, 0:48].rearrange("p (c m) -> p c m", c=16)), inc=VE)
            pb1["a"] = (VE, p.cnt[VE])
        ve_mgn = p.cnt[VE]

        p.wait("vector", VE, ve_mgn)
        ve_conv = {}
        for k in (0, 1, 2, 3):
            for dc in range(DIC):
                base = dc * (TT + 3)
                dst = xconv[:, dc * TT:(dc + 1) * TT]
                if k == 0:
                    p.wait("vector", *ac_xp[dc])
                    p.ts("vector", dst, xproj[:, base:base + TT],
                         convw_sb[:, dc * 4:dc * 4 + 1], convb_sb[:, dc:dc + 1],
                         ALU.mult, ALU.add)
                else:
                    p.stt("vector", dst, xproj[:, base + k:base + k + TT],
                          convw_sb[:, dc * 4 + k:dc * 4 + k + 1], dst,
                          ALU.mult, ALU.add, inc=(VE if k == 3 else None))
                if k == 3:
                    ve_conv[dc] = p.cnt[VE]
        ac_sil = {}
        for dc in range(DIC):
            p.wait("scalar", VE, ve_conv[dc])
            sl = xconv[:, dc * TT:(dc + 1) * TT]
            p.act("scalar", sl, sl, ACT.Silu, inc=AC)
            ac_sil[dc] = p.cnt[AC]
        ac_conv = p.cnt[AC]

        p.wait("sync", AC, ac_conv)
        p.dma("sync", XCr[:, :, T0:T0 + TT],
              xconv[:].rearrange("p (c t) -> p c t", c=DIC), SPL)
        spl_xc = p.cnt[SPL]

        if t == nwq and ve_wbc:
            p.wait("tensor", VE, ve_wbc)
        for dc in range(DIC):
            p.wait("tensor", AC, ac_sil[dc])
            rhs = xconv[:, dc * TT:(dc + 1) * TT]
            p.mm(p_x[:], wxb[:, dc * 33:dc * 33 + 16], rhs,
                 start=(dc == 0), stop=(dc == DIC - 1),
                 inc=(TE if dc == DIC - 1 else None))
        te_B = p.cnt[TE]
        for dc in range(DIC):
            rhs = xconv[:, dc * TT:(dc + 1) * TT]
            p.mm(p_dt2[0:1, :], wxb[:, dc * 33 + 32:dc * 33 + 33], rhs,
                 start=(dc == 0), stop=(dc == DIC - 1),
                 inc=(TE if dc == DIC - 1 else None))
        te_dt = p.cnt[TE]

        p.wait("vector", TE, te_B)
        p.cp("vector", braw[:], p_x[:], inc=VE)
        ve_braw = p.cnt[VE]
        p.wait("vector", TE, te_dt)
        p.ts("vector", dtch[0:1, :], p_dt2[0:1, :], 0.0, None, ALU.min,
             inc=VE)
        p.wait("scalar", VE, p.cnt[VE])
        p.act("scalar", dtch[0:1, :], dtch[0:1, :], ACT.Exp)
        p.act("scalar", dtch[0:1, :], dtch[0:1, :], ACT.Ln,
              bias=onecol[0:1, 0:1], inc=AC)
        p.wait("vector", AC, p.cnt[AC])
        p.ts("vector", dtch[0:1, :], dtch[0:1, :], 0.1, 0.001,
             ALU.min, ALU.max, inc=VE)
        ve_dtc = p.cnt[VE]

        p.wait("tensor", VE, ve_braw)
        for dc in range(DIC):
            rhs = xconv[:, dc * TT:(dc + 1) * TT]
            p.mm(p_x[:], wxb[:, dc * 33 + 16:dc * 33 + 32], rhs,
                 start=(dc == 0), stop=(dc == DIC - 1),
                 inc=(TE if dc == DIC - 1 else None))
        te_C = p.cnt[TE]
        p.wait("tensor", VE, ve_dtc)
        p.mm(p_dt2[:], onesf[0:1, 0:16], dtch[0:1, :], True, True, inc=TE)
        te_bc = p.cnt[TE]

        p.wait("vector", TE, te_bc)
        if t == 0:
            p.wait("vector", VE, ve_acol)
        p.cp("vector", dtsb[:], p_dt2[:])
        p.ts("vector", dta[:], p_dt2[:], acol[0:16, 0:1], -20.0,
             ALU.mult, ALU.max)
        p.tt("vector", bbar[:], braw[:], dtsb[:], ALU.mult)
        p.ts("vector", bbar[:], bbar[:], 10.0, -10.0, ALU.min, ALU.max)
        p.wait("vector", TE, te_C)
        p.cp("vector", cevac[:], p_x[:])
        p.scan("vector", s16[:], dta[:], dta[:], 0.0, ALU.add, ALU.bypass)
        p.op("vector", [fbc], [s16], lambda e: e.tensor_copy(
            fbc[:].rearrange("p (c t) -> p c t", c=NCH),
            s16[:, 31::64].unsqueeze(2).broadcast_to((16, NCH, CH))))
        p.mset("vector", sp8[:, 0:1], 0.0)
        p.op("vector", [sp8], [s16], lambda e: e.tensor_copy(
            sp8[:, 1:8], s16[:, 63::64][:, 0:7]))
        p.tt("vector", ccb[:, 0:8], s16[:, 63::64], s16[:, 31::64],
             ALU.subtract)
        p.tt("vector", ccb[:, 8:16], s16[:, 31::64], sp8[:], ALU.subtract)
        p.tt("vector", ccb[:, 16:24], s16[:, 63::64], sp8[:], ALU.subtract)
        p.tt("vector", eargB[:], fbc[:], s16[:], ALU.subtract)
        p.ts("vector", eargB[:], eargB[:], 87.0, -20.10126823437577,
             ALU.min, ALU.add)
        p.tt("vector", eargC[:], s16[:], fbc[:], ALU.subtract)
        p.ts("vector", eargC[:], eargC[:], 87.0, None, ALU.min)
        p.nopinc("vector", VE)
        p.wait("scalar", VE, p.cnt[VE])
        p.act("scalar", ecb[:], ccb[:], ACT.Exp)
        p.act("scalar", eargB[:], eargB[:], ACT.Exp)
        p.act("scalar", eargC[:], eargC[:], ACT.Exp, inc=AC)
        p.wait("vector", AC, p.cnt[AC])
        p.tt("vector", bpp[:], bbar[:], eargB[:], ALU.mult)
        p.tt("vector", cpk[:], cevac[:], eargC[:], ALU.mult)
        p.op("vector", [rhsm], [cpk], lambda e: e.tensor_copy(
            rhsm[:].rearrange("p (c k) -> p c k", k=AUGK)[:, :, 0:CH],
            cpk[:].rearrange("p (c t) -> p c t", c=NCH)))
        for c in range(NCH):
            p.ts("vector", rhsm[:, c * AUGK + CH:(c + 1) * AUGK], i16b[:],
                 ecb[:, c:c + 1], None, ALU.mult)
            p.ts("vector", diag2a[:, c * 16:(c + 1) * 16], i16b[:],
                 ecb[:, 8 + c:9 + c], None, ALU.mult)
        p.tt("vector", scn8[:, 0:1], ecb[:, 16:17], ppcol[:], ALU.mult)
        p.cp("vector", scn8[:, 1:8], ecb[:, 17:24])
        p.scan("vector", scn8[:, 8:16], scn8[:, 0:8], scn8[:, 0:8], 1.0,
               ALU.mult, ALU.bypass)
        p.cp("vector", scn8[:, 16:17], ppcol[:])
        p.cp("vector", scn8[:, 17:24], scn8[:, 8:15])
        p.cp("vector", ppcol[:], scn8[:, 15:16])
        p.tt("vector", scn8[:, 0:8], ecb[:, 8:16], scn8[:, 16:24], ALU.mult)
        for c in range(NCH):
            p.ts("vector", cgbuf[:, T0 + c * CH:T0 + (c + 1) * CH],
                 cpk[:, c * CH:(c + 1) * CH], scn8[:, c:c + 1], None, ALU.mult)
        p.nopinc("vector", VE)
        ve_prep = p.cnt[VE]

        if n_tiles == NTILE and 2 <= t:
            p.wait("scalar", VE, ve_wbc)
            p.wait("gpsimd", VE, ve_wbc)
            emit_quant_tile(all_chunks[8 * (t - 2):8 * (t - 1)], 8 * (t - 2))

        def issue_reload(w):
            p.wait("sync", SPL, None)
            p.wait("sync", SS, None)
            p.wait("sync", SSB, None)
            p.dma("sync", uw[:, (w % 2) * DI:(w % 2) * DI + DI],
                  SP_XC[:, T0 + w * 128:T0 + w * 128 + 128], TR,
                  transpose=True)

        def issue_copy(c):
            gc_ = t * NCH + c
            pr = gc_ % 2
            p.wait("sync", TR, None)
            if te_aug_end[0]:
                p.wait("sync", TE, te_aug_end[0])
            p.dma("sync", rhsa[0:64, pr * DI:(pr + 1) * DI],
                  uw[(c % 2) * 64:(c % 2) * 64 + 64,
                     ((c // 2) % 2) * DI:((c // 2) % 2) * DI + DI],
                  SS if pr == 0 else SSB)

        def emit_mb(c, first):
            if first:
                p.wait("tensor", VE, ve_prep)
            else:
                p.wait("tensor", VE, ve_taug[c - 1])
            p.mm(p_m1[:, :], bpp[:, c * CH:(c + 1) * CH],
                 rhsm[:, c * AUGK:(c + 1) * AUGK], True, True, tp=(0, 0))
            p.mm(p_m2[64:80, :], diag2a[:, c * 16:(c + 1) * 16],
                 rhsm[:, c * AUGK:(c + 1) * AUGK], True, True, tp=(0, 64),
                 inc=TE)
            te_mb[c] = p.cnt[TE]

        ve_taug = {}
        te_mb = {}
        emit_mb(0, True)
        issue_reload(0)
        issue_copy(0)
        issue_reload(1)
        for c in range(NCH):
            w0 = c * CH
            gci = t * NCH + c
            par, npar = gci % 2, (gci + 1) % 2

            tb = (c % 2) * AUGK
            p.wait("vector", TE, te_mb[c])
            p.tt("vector", taug[0:64, tb:tb + AUGK], p_m1[:, :], maskb[:],
                 ALU.mult)
            p.cp("vector", taug[64:80, tb:tb + AUGK], p_m2[64:80, :], inc=VE)
            ve_taug[c] = p.cnt[VE]

            if c + 1 < NCH:
                issue_copy(c + 1)
            if c % 2 == 1 and (c + 3) // 2 < NCH // 2:
                issue_reload((c + 3) // 2)
            if c + 1 < NCH:
                emit_mb(c + 1, False)

            p.wait("tensor", VE, ve_taug[c])
            p.wait("tensor", SS if par == 0 else SSB, None)
            p.wait("vector", SPL, None)
            yb = par * DI
            for n in range(4):
                bk = ("g", "h", "a", "b")[n]
                bank = (p_g, p_h, p_a, p_b)[n]
                if pb1[bk] is not None:
                    p.wait("tensor", *pb1[bk])
                p.mm(bank[0:80, :], taug[:, tb:tb + AUGK],
                     rhsa[:, par * DI + n * 512:par * DI + (n + 1) * 512],
                     True, True, inc=TE)
                p.wait("vector", TE, p.cnt[TE])
                p.ts("vector", rhsa[64:80, npar * DI + n * TT:
                                    npar * DI + (n + 1) * TT],
                     bank[64:80, :], 100.0, -100.0, ALU.min, ALU.max)
                p.cp("vector", ysb[:, yb + n * TT:yb + (n + 1) * TT],
                     bank[0:64, :], inc=VE)
                pb1[bk] = (VE, p.cnt[VE])
            te_aug_end[0] = p.cnt[TE]
            p.wait("sync", VE, None)
            p.dma("sync", SP_Y[T0 + w0:T0 + w0 + CH, :],
                  ysb[:, yb:yb + DI], SPL)

        per = (96 + nwq - 1) // nwq
        if t < nwq:
            emit_wq_batch(t * per, min((t + 1) * per, 96))
        if t == nwq - 1:
            p.wait("vector", TE, None)
            p.cp("vector", kcol[:, 1:4], wsum[:, 0:3], inc=VE)
            p.wait("tensor", VE, None)
            p.mm(p_a[0:1, 0:3], kcol[:, 0:1], kcol[:, 1:4], True, True,
                 inc=TE)
            p.wait("vector", TE, None)
            p.cp("vector", stat[0:1, 4:7], p_a[0:1, 0:3], inc=VE)
            p.wait("tensor", VE, None)
            p.mm(p_a[0:128, 4:7], onesf[0:1, 0:128], stat[0:1, 4:7],
                 True, True, inc=TE)
            p.wait("vector", TE, None)
            p.cp("vector", wsum[:, 5:8], p_a[0:128, 4:7], inc=VE)
            pb1["a"] = (VE, p.cnt[VE])
            p.ts("vector", wmean[:, 0:3], wsum[:, 5:8], 1.0 / (FF * DM),
                 EPS, ALU.mult, ALU.max)
            p.recip("vector", swcol[:, 0:1], wmean[:, 0:1])
            p.recip("vector", swcol[:, 1:2], wmean[:, 1:2])
            p.recip("vector", swcol[:, 2:3], wmean[:, 2:3], inc=VE)
            ve_wbc = p.cnt[VE]

    par_end = (NTILE * NCH) % 2
    p.wait("vector", VE, None)
    p.cp("vector", hpack[:, 0:DI],
         rhsa[64:80, par_end * DI:(par_end + 1) * DI])
    p.cp("vector", hpack[:, DI:DI + 1], ppcol[:], inc=VE)
    p.wait("sync", VE, None)
    p.dma("sync", CC_IN, hpack[:], SPL)
    cc_ld = p.cnt[SPL]

    p.wait("sync", SPL, None)
    p.wait("sync", SPLZ[0], None)
    p.wait("sync", SPLZ[1], None)
    p.wait("sync", SS, None)
    p.wait("sync", TR, None)
    p.wait("sync", LD, None)

    p.wait("gpsimd", SPL, cc_ld)
    if sim_cc:
        for j in range(4):
            p.dma("gpsimd", CC_OUT[j * 16:(j + 1) * 16, :], CC_IN, GP)
        p.wait("gpsimd", GP, None)
        p.nopinc("gpsimd", GC)
    else:
        p.op("gpsimd", [], [], lambda e: e.collective_compute(
            "AllGather", ALU.bypass,
            replica_groups=[[0, 1, 2, 3], [4, 5, 6, 7]],
            ins=[CC_IN], outs=[CC_OUT]), inc=GC)
    gc_cc = p.cnt[GC]

    p.barrier(("sync", "vector", "scalar", "tensor"), BAR)
    p.flush()
    ph1.close()

    phx = ExitStack()
    p_c = phx.enter_context(nc.psum_tensor("p_c", [128, TT], F32))
    p_d = phx.enter_context(nc.psum_tensor("p_d", [128, TT], F32))
    PBANKS = (p_a, p_b, p_c, p_d)
    pb_marks = [None, None, None, None]


    ph2 = ExitStack()
    sb2 = lambda n, s, d: ph2.enter_context(nc.sbuf_tensor(n, s, d))
    woutb = sb2("woutb", [128, DIC * DM], BF16)
    yfm = sb2("yfm", [128, 2 * DIC * TT], BF16)
    xcr = sb2("xcr", [128, DIC * TT], BF16)
    zr = sb2("zr", [128, DIC * TT], BF16)
    ygate = sb2("ygate", [128, DIC * TT], BF16)
    resb = sb2("resb", [128, DMC * TT], BF16)
    rest4 = sb2("rest4", [128, 4 * DM], BF16)
    x1t = sb2("x1t", [128, 4 * DM], BF16)
    aqt = sb2("aqt", [128, 4 * DM], BF16)
    sqt = sb2("sqt", [128, DM], F32)
    gsh = sb2("gsh", [16, DI + 1], F32)

    p.wait("sync", GC, gc_cc)
    p.mset("vector", hin[:], 0.0)
    for j in range(3):
        p.wait("sync", VE, None)
        p.dma("sync", gsh[:], CC_OUT[j * 16:(j + 1) * 16, :], LD)
        p.wait("vector", LD)
        p.ts("vector", stat[0:16, 9:10], gsh[:, DI:DI + 1],
             cmb_sb[:, 3 + j:4 + j], None, ALU.mult)
        p.tt("vector", stat[0:16, 9:10], stat[0:16, 9:10],
             cmb_sb[:, 6 + j:7 + j], ALU.add)
        p.ts("vector", gsh[:, 0:DI], gsh[:, 0:DI], cmb_sb[:, j:j + 1],
             None, ALU.mult)
        p.stt("vector", hin[:], hin[:], stat[0:16, 9:10], gsh[:, 0:DI],
              ALU.mult, ALU.add)
        p.nopinc("vector", VE)
    p.ts("vector", hin[:], hin[:], 100.0, -100.0, ALU.min, ALU.max)
    p.cp("vector", hinb[:], hin[:], inc=VE)
    ve_hin = p.cnt[VE]

    p.dma("gpsimd", woutb[:].rearrange("p (c f) -> p c f", c=DIC), WOUTr, GW)
    gw_wo = p.cnt[GW]

    p.wait("gpsimd", VE, None)
    p.dma("gpsimd", xbf[:].rearrange("p (c t) -> p c t", c=DMC),
          XTr[:, :, 3:3 + TT], GW)

    def b2_loads(tl, ve_guard):
        T0l = tl * TT
        pb = (tl % 2) * DIC * TT
        if ve_guard:
            p.wait("sync", VE, ve_guard)
        p.dma("sync", xcr[:].rearrange("p (c t) -> p c t", c=DIC),
              XCr[:, :, T0l:T0l + TT], LD)
        p.dma("sync", zr[:].rearrange("p (c t) -> p c t", c=DIC),
              Zr_[:, :, T0l:T0l + TT], LD)
        for dc in range(DIC):
            p.dma("sync", yfm[:, pb + dc * TT:pb + (dc + 1) * TT],
                  SP_Y[T0l:T0l + TT, dc * 128:(dc + 1) * 128], TR,
                  transpose=True)

    b2_loads(0, 0)
    for t in range(n_tiles):
        T0 = t * TT
        pb = (t % 2) * DIC * TT
        p.wait("vector", LD)
        p.wait("vector", TR)
        p.wait("vector", GW)
        p.nopinc("vector", VE)
        ve_rel = p.cnt[VE]

        p.wait("tensor", VE, ve_rel)
        if t == 0:
            p.wait("tensor", VE, ve_hin)
        for g4 in range(DIC // 4):
            for i in range(4):
                dc = g4 * 4 + i
                if pb_marks[i] is not None:
                    p.wait("tensor", *pb_marks[i])
                p.mm(PBANKS[i][:], hinb[:, dc * 128:(dc + 1) * 128],
                     cgbuf[:, T0:T0 + TT], True, True, inc=TE)
            p.wait("vector", TE, p.cnt[TE])
            for i in range(4):
                dc = g4 * 4 + i
                sl = slice(dc * TT, (dc + 1) * TT)
                p.tt("vector", ygate[:, sl],
                     yfm[:, pb + dc * TT:pb + (dc + 1) * TT],
                     PBANKS[i][:], ALU.add, inc=VE)
                pb_marks[i] = (VE, p.cnt[VE])
        p.tt("vector", ygate[:], ygate[:], zr[:], ALU.mult)
        for dc in range(DIC):
            sl = slice(dc * TT, (dc + 1) * TT)
            p.stt("vector", ygate[:, sl], xcr[:, sl],
                  dpar_sb[:, dc:dc + 1], ygate[:, sl], ALU.mult, ALU.add)
        p.nopinc("vector", VE)

        ve_gate = p.cnt[VE]
        p.wait("tensor", VE, ve_gate)
        if t == 0:
            p.wait("tensor", GW, None)
        for dmc in range(DMC):
            bi = dmc % 4
            bank = PBANKS[bi]
            if pb_marks[bi] is not None:
                p.wait("tensor", *pb_marks[bi])
            for dc in range(DIC):
                lw = woutb[:, dc * 1024 + dmc * 128: dc * 1024 + (dmc + 1) * 128]
                p.mm(bank[:], lw, ygate[:, dc * TT:(dc + 1) * TT],
                     start=(dc == 0), stop=(dc == DIC - 1),
                     inc=(TE if dc == DIC - 1 else None))
            p.wait("vector", TE, p.cnt[TE])
            p.tt("vector", resb[:, dmc * TT:(dmc + 1) * TT], bank[:],
                 xbf[:, dmc * TT:(dmc + 1) * TT], ALU.add, inc=VE)
            pb_marks[bi] = (VE, p.cnt[VE])
        ve_res = p.cnt[VE]
        p.wait("sync", VE, None)
        p.dma("sync", RESr[:, :, T0:T0 + TT],
              resb[:].rearrange("p (c t) -> p c t", c=DMC), SPL)
        spl_res = p.cnt[SPL]
        if t + 1 < n_tiles:
            b2_loads(t + 1, ve_gate)

        p.wait("sync", SPL, spl_res)
        p.wait("sync", VE, None)
        for w in range(4):
            p.dma("sync", rest4[:, w * DM:(w + 1) * DM],
                  SP_RES[:, T0 + w * 128:T0 + w * 128 + 128],
                  TR, transpose=True)
        p.wait("vector", TR)
        for w in range(4):
            p.red("vector", lnst[:, w:w + 1], rest4[:, w * DM:(w + 1) * DM],
                  ALU.add)
        p.nopinc("vector", VE)
        p.wait("scalar", VE, p.cnt[VE])
        for w in range(4):
            p.act("scalar", sqt[:], rest4[:, w * DM:(w + 1) * DM], ACT.Square,
                  accum=lnst[:, 4 + w:5 + w], inc=AC)
        p.wait("vector", AC, p.cnt[AC])
        p.ts("vector", lnst[:, 8:12], lnst[:, 0:4], 1.0 / DM, None, ALU.mult)
        p.ts("vector", lnst[:, 12:16], lnst[:, 4:8], 1.0 / DM, None, ALU.mult)
        p.tt("vector", lnst[:, 16:20], lnst[:, 8:12], lnst[:, 8:12], ALU.mult)
        p.tt("vector", lnst[:, 20:24], lnst[:, 12:16], lnst[:, 16:20],
             ALU.subtract, inc=VE)
        p.wait("scalar", VE, p.cnt[VE])
        p.act("scalar", lnst[:, 24:28], lnst[:, 20:24], ACT.Ln,
              bias=epscol[:, 0:1])
        p.act("scalar", lnst[:, 28:32], lnst[:, 24:28], ACT.Exp,
              scale=nhalf[:, 0:1], inc=AC)
        p.wait("vector", AC, p.cnt[AC])
        p.wait("vector", SPL, None)
        for w in range(4):
            p.ts("vector", x1t[:, w * DM:(w + 1) * DM],
                 rest4[:, w * DM:(w + 1) * DM], lnst[:, 8 + w:9 + w],
                 lnst[:, 28 + w:29 + w], ALU.subtract, ALU.mult)
        for w in range(4):
            p.red("vector", lnst[:, 32 + w:33 + w],
                  x1t[:, w * DM:(w + 1) * DM], ALU.max, absval=True)
        p.ts("vector", lnst[:, 36:40], lnst[:, 32:36], EPS, None, ALU.max)
        p.recip("vector", lnst[:, 40:44], lnst[:, 36:40])
        p.ts("vector", lnst[:, 44:48], lnst[:, 40:44], 127.0, None, ALU.mult)
        p.ts("vector", invsa[:, t * 4:t * 4 + 4], lnst[:, 36:40],
             1.0 / 127.0, None, ALU.mult)
        p.nopinc("vector", VE)
        ve_ln = p.cnt[VE]
        p.wait("scalar", VE, ve_ln)
        p.wait("scalar", SPL, None)
        for w in range(4):
            p.act("scalar", sqt[:], x1t[:, w * DM:(w + 1) * DM], ACT.Identity,
                  bias=cmag[:, 0:1], scale=lnst[:, 44 + w:45 + w])
            p.act("scalar", aqt[:, w * DM:(w + 1) * DM], sqt[:], ACT.Identity,
                  bias=cmagn[:, 0:1], inc=(AC if w == 3 else None))
        p.wait("sync", VE, None)
        p.wait("sync", AC, None)
        p.dma("sync", SP_X1[T0:T0 + TT, :].rearrange("(w p) f -> p w f", p=128),
              x1t[:].rearrange("p (w f) -> p w f", w=4), SPL)
        p.dma("sync", SP_AQ[T0:T0 + TT, :].rearrange("(w p) f -> p w f", p=128),
              aqt[:].rearrange("p (w f) -> p w f", w=4), SPL)

        if t + 1 < n_tiles:
            p.wait("gpsimd", VE, ve_res)
            p.dma("gpsimd", xbf[:].rearrange("p (c t) -> p c t", c=DMC),
                  XTr[:, :, 3 + T0 + TT:3 + T0 + 2 * TT], GW)
        b1q = 48 if n_tiles == NTILE else 0
        pg = (96 - b1q + n_tiles - 1) // n_tiles
        emit_quant_tile(all_chunks[b1q + pg * t:
                                   min(b1q + pg * (t + 1), 96)], b1q + pg * t)

    p.barrier(("sync", "vector", "scalar", "tensor"), BAR)
    p.flush()
    ph2.close()
    ph12.close()
    phxs.close()

    ph3 = ExitStack()
    sb3 = lambda n, s, d: ph3.enter_context(nc.sbuf_tensor(n, s, d))
    wgub = sb3("wgub", [128, DMC * 2 * FF], BF16)
    aqf = sb3("aqf", [128, 2 * DMC * TT], BF16)
    rgb = sb3("rgb", [128, FF], BF16)
    ub = sb3("ub", [128, FF], BF16)
    hb = sb3("hb", [128, 2 * FF], BF16)
    hqp = sb3("hqp", [128, TT], F32)

    p.wait("sync", GP, None)
    p.wait("sync", SPL, None)
    ve_rgb_free = 0
    ve_ub_free = 0
    p.wait("sync", TR, None)
    p.wait("sync", LD, None)
    p.wait("sync", GW, None)
    for c in range(DMC):
        p.dma("sync", wgub[:, c * 2 * FF:(c + 1) * 2 * FF], QGUr[c], LD)
    ld_gu = p.cnt[LD]
    p.wait("tensor", LD, ld_gu)

    p.tt("vector", lnst[:, 40:41], wmean[:, 0:1], wmean[:, 0:1], ALU.mult)
    p.tt("vector", lnst[:, 40:41], lnst[:, 40:41], wmean[:, 1:2], ALU.mult)
    nsb = 4 * n_tiles
    p.tt("vector", lnst[:, 0:nsb], invsa[:, 0:nsb], invsa[:, 0:nsb], ALU.mult)
    p.tt("vector", lnst[:, 0:nsb], lnst[:, 0:nsb], invsa[:, 0:nsb], ALU.mult)
    p.ts("vector", lnst[:, 0:nsb], lnst[:, 0:nsb], lnst[:, 40:41], None,
         ALU.mult)

    pend = None

    def q2_piece(grp):
        pT0, ps0, ppar, pve = pend
        sl = slice(ppar * FF + grp * TT, ppar * FF + (grp + 1) * TT)
        if grp == 0:
            p.wait("scalar", VE, pve)
        p.act("scalar", hqp[:], hb[:, sl], ACT.Identity, bias=cmag[:, 0:1],
              scale=kcol[:, 6 + ppar:7 + ppar])
        p.act("scalar", hb[:, sl], hqp[:], ACT.Identity, bias=cmagn[:, 0:1],
              inc=(AC if grp == 7 else None))
        if grp == 7:
            p.wait("sync", AC, p.cnt[AC])
            p.dma("sync", SP_AQ2[pT0 + ps0:pT0 + ps0 + 128, :],
                  hb[:, ppar * FF:(ppar + 1) * FF], SPL)

    def b3_aq(tl):
        T0l = tl * TT
        ab = (tl % 2) * DMC * TT
        for dm in range(DMC):
            p.dma("sync", aqf[:, ab + dm * TT:ab + (dm + 1) * TT],
                  SP_AQ[T0l:T0l + TT, dm * 128:(dm + 1) * 128], TR,
                  transpose=True)

    sidx = 0
    te_end = [0, 0]
    b3_aq(0)
    for t in range(n_tiles):
        T0 = t * TT
        ab = (t % 2) * DMC * TT
        p.wait("tensor", TR, None)
        for sub in range(4):
            s0 = sub * 128
            par = sidx % 2
            hbs = hb[:, par * FF:(par + 1) * FF]
            for grp in range(8):
                bis = ((grp % 2) * 2, (grp % 2) * 2 + 1)
                for n in range(2):
                    if pb_marks[bis[n]] is not None:
                        p.wait("tensor", *pb_marks[bis[n]])
                for dm in range(DMC):
                    lw = aqf[:, ab + dm * TT + s0:ab + dm * TT + s0 + 128]
                    for n in range(2):
                        f = grp * 2 + n
                        bank = PBANKS[bis[n]]
                        p.mm(bank[:], lw,
                             wgub[:, dm * 2 * FF + f * TT:
                                  dm * 2 * FF + (f + 1) * TT],
                             start=(dm == 0), stop=(dm == DMC - 1),
                             inc=(TE if (dm == DMC - 1 and n == 1) else None))
                te_g = p.cnt[TE]
                for n in range(2):
                    f = grp * 2 + n
                    bank = PBANKS[bis[n]]
                    if f < 8:
                        p.wait("scalar", TE, te_g)
                        p.wait("scalar", VE, ve_rgb_free)
                        p.act("scalar", rgb[:, f * TT:(f + 1) * TT], bank[:],
                              ACT.Relu, inc=AC)
                        pb_marks[bis[n]] = (AC, p.cnt[AC])
                    else:
                        p.wait("vector", TE, te_g)
                        p.wait("vector", VE, ve_ub_free)
                        p.cp("vector", ub[:, (f - 8) * TT:(f - 7) * TT],
                             bank[:], inc=VE)
                        pb_marks[bis[n]] = (VE, p.cnt[VE])
                if pend is not None:
                    q2_piece(grp)
            p.wait("vector", AC, None)
            p.wait("vector", SPL, None)
            p.tt("vector", hbs, rgb[:], rgb[:], ALU.mult, inc=VE)
            ve_rgb_free = p.cnt[VE]
            p.tt("vector", hbs, hbs, ub[:], ALU.mult, inc=VE)
            ve_ub_free = p.cnt[VE]
            p.red("vector", kcol[:, 0:1], hbs, ALU.max, absval=True)
            k1 = lnst[:, t * 4 + sub:t * 4 + sub + 1]
            p.tt("vector", kcol[:, 3:4], kcol[:, 0:1], k1, ALU.mult)
            p.ts("vector", kcol[:, 3:4], kcol[:, 3:4], EPS, None, ALU.max)
            p.recip("vector", kcol[:, 4:5], kcol[:, 3:4])
            p.tt("vector", kcol[:, 5:6], kcol[:, 4:5], k1, ALU.mult)
            p.ts("vector", kcol[:, 6 + par:7 + par], kcol[:, 5:6], 127.0,
                 None, ALU.mult, inc=VE)
            ve_k = p.cnt[VE]
            p.ts("vector", invs2[:, t * 4 + sub:t * 4 + sub + 1], kcol[:, 3:4],
                 1.0 / 127.0, None, ALU.mult, inc=VE)
            pend = (T0, s0, par, ve_k)
            sidx += 1
        if t + 1 < n_tiles:
            p.wait("sync", TE, te_end[(t + 1) % 2])
            b3_aq(t + 1)
        te_end[t % 2] = p.cnt[TE]

    for grp in range(8):
        q2_piece(grp)
    pend = None

    p.barrier(("sync", "vector", "scalar", "tensor"), BAR)
    p.flush()
    ph3.close()

    ph4 = ExitStack()
    sb4 = lambda n, s, d: ph4.enter_context(nc.sbuf_tensor(n, s, d))
    wdb = sb4("wdb", [128, FFC * DM], BF16)
    aq2f = sb4("aq2f", [128, 2 * FFC * TT], BF16)
    x1r = sb4("x1r", [128, 2 * 4 * DM], BF16)
    dnb4 = sb4("dnb4", [128, 4 * DM], F32)
    ost = sb4("ost", [128, 4 * DM], F32)

    p.wait("sync", SPL, None)
    p.wait("sync", TR, None)
    p.wait("sync", LD, None)
    p.wait("sync", GP, None)
    for c in range(FFC):
        p.dma("sync", wdb[:, c * DM:(c + 1) * DM], QDr[c], LD)
    ld_d = p.cnt[LD]
    p.wait("tensor", LD, ld_d)

    def b4_loads(tl, ve_guard, te_guard):
        T0l = tl * TT
        ab = (tl % 2) * FFC * TT
        xb = (tl % 2) * 4 * DM
        if ve_guard:
            p.wait("sync", VE, ve_guard)
        p.dma("sync", x1r[:, xb:xb + 4 * DM].rearrange("p (w f) -> p w f", w=4),
              SP_X1[T0l:T0l + TT, :].rearrange("(w p) f -> p w f", p=128), LD)
        if te_guard:
            p.wait("sync", TE, te_guard)
        for ic in range(FFC):
            p.dma("sync", aq2f[:, ab + ic * TT:ab + (ic + 1) * TT],
                  SP_AQ2[T0l:T0l + TT, ic * 128:(ic + 1) * 128],
                  TR, transpose=True)

    te_end = [0, 0]
    ve_x1 = [0, 0]
    b4_loads(0, 0, 0)
    for t in range(n_tiles):
        T0 = t * TT
        ab = (t % 2) * FFC * TT
        xb = (t % 2) * 4 * DM
        p.wait("tensor", TR, None)
        for sub in range(4):
            s0 = sub * 128
            bis = ((sub % 2) * 2, (sub % 2) * 2 + 1)
            for n in range(2):
                if pb_marks[bis[n]] is not None:
                    p.wait("tensor", *pb_marks[bis[n]])
            for ic in range(FFC):
                lw = aq2f[:, ab + ic * TT + s0:ab + ic * TT + s0 + 128]
                for n in range(2):
                    bank = PBANKS[bis[n]]
                    p.mm(bank[:], lw,
                         wdb[:, ic * DM + n * TT:ic * DM + (n + 1) * TT],
                         start=(ic == 0), stop=(ic == FFC - 1),
                         inc=(TE if (ic == FFC - 1 and n == 1) else None))
            te_d = p.cnt[TE]
            p.wait("vector", TE, te_d)
            if sub == 0:
                p.wait("vector", LD, None)
                p.wait("vector", STO, None)
            iv = invs2[:, t * 4 + sub:t * 4 + sub + 1]
            wsl = slice(sub * DM, (sub + 1) * DM)
            for n in range(2):
                bank = PBANKS[bis[n]]
                p.ts("vector", dnb4[:, sub * DM + n * TT:
                                sub * DM + (n + 1) * TT], bank[:], iv,
                     wmean[:, 2:3], ALU.mult, ALU.mult, inc=VE)
                pb_marks[bis[n]] = (VE, p.cnt[VE])
            p.tt("vector", dnb4[:, wsl], dnb4[:, wsl], x1r[:, xb + wsl.start:
                 xb + wsl.stop], ALU.add,
                 inc=(VE if sub == 3 else None))
        ve_x1[t % 2] = p.cnt[VE]
        if t + 1 < n_tiles:
            b4_loads(t + 1, ve_x1[(t + 1) % 2], te_end[(t + 1) % 2])
        te_end[t % 2] = p.cnt[TE]

        for w in range(4):
            p.red("vector", lnst[:, w:w + 1], dnb4[:, w * DM:(w + 1) * DM],
                  ALU.add)
        p.nopinc("vector", VE)
        p.wait("scalar", VE, p.cnt[VE])
        p.wait("scalar", STO, None)
        for w in range(4):
            p.act("scalar", ost[:, w * DM:(w + 1) * DM],
                  dnb4[:, w * DM:(w + 1) * DM], ACT.Square,
                  accum=lnst[:, 4 + w:5 + w], inc=AC)
        p.wait("vector", AC, p.cnt[AC])
        p.ts("vector", lnst[:, 8:12], lnst[:, 0:4], 1.0 / DM, None, ALU.mult)
        p.ts("vector", lnst[:, 12:16], lnst[:, 4:8], 1.0 / DM, None, ALU.mult)
        p.tt("vector", lnst[:, 16:20], lnst[:, 8:12], lnst[:, 8:12], ALU.mult)
        p.tt("vector", lnst[:, 20:24], lnst[:, 12:16], lnst[:, 16:20],
             ALU.subtract, inc=VE)
        p.wait("scalar", VE, p.cnt[VE])
        p.act("scalar", lnst[:, 24:28], lnst[:, 20:24], ACT.Ln,
              bias=epscol[:, 0:1])
        p.act("scalar", lnst[:, 28:32], lnst[:, 24:28], ACT.Exp,
              scale=nhalf[:, 0:1], inc=AC)
        p.wait("vector", AC, p.cnt[AC])
        for w in range(4):
            p.ts("vector", ost[:, w * DM:(w + 1) * DM],
                 dnb4[:, w * DM:(w + 1) * DM], lnst[:, 8 + w:9 + w],
                 lnst[:, 28 + w:29 + w], ALU.subtract, ALU.mult)
        p.nopinc("vector", VE)
        p.wait("gpsimd", VE, None)
        for w in range(4):
            p.dma("gpsimd", OUT[T0 + w * 128:T0 + w * 128 + 128, :],
                  ost[:, w * DM:(w + 1) * DM], STO)

    p.wait("sync", STO, None)
    p.flush()
    ph4.close()
    phx.close()
    ctx.close()
    return nc


_CACHE = {}


def kernel(L0_attn_x, L0_attn_W_in, L0_attn_conv_w, L0_attn_conv_b,
           L0_attn_W_xproj, L0_attn_A_log, L0_attn_D_param, L0_attn_W_out,
           L0_ffn_w_gate, L0_ffn_w_up, L0_ffn_w_down):
    x = np.asarray(L0_attn_x, np.float32)
    f32 = lambda a: np.ascontiguousarray(np.asarray(a, np.float32))

    xpad = np.pad(x, ((0, 0), (3, 0), (0, 0)))
    xt = np.ascontiguousarray(xpad.transpose(0, 2, 1))

    w_inT = f32(np.asarray(L0_attn_W_in).T)
    wxT = f32(np.asarray(L0_attn_W_xproj).T)
    w_outT = f32(np.asarray(L0_attn_W_out).T)
    wguT = f32(np.concatenate([np.asarray(L0_ffn_w_gate).T,
                               np.asarray(L0_ffn_w_up).T], axis=1))
    wdT = f32(np.asarray(L0_ffn_w_down).T)
    convw = f32(np.asarray(L0_attn_conv_w).reshape(16, 128, 4)
                .transpose(1, 0, 2).reshape(128, 64))
    convb = f32(np.asarray(L0_attn_conv_b).reshape(16, 128).T)
    dparam = f32(np.asarray(L0_attn_D_param).reshape(16, 128).T)
    alog48 = f32(np.tile(np.asarray(L0_attn_A_log), 3)[:, None])

    if "nc" not in _CACHE:
        _CACHE["nc"] = build_nc()
    nc = _CACHE["nc"]
    in_maps = _stage(x, xt, w_inT, wxT, w_outT, wguT, wdT, convw, convb,
                     dparam, alog48)
    res = run_bass_kernel_spmd(nc, in_maps, core_ids=list(range(8)))
    out = np.empty((B, L, DM), np.float32)
    for k in range(8):
        b, q = k // 4, k % 4
        out[b, q * TOK:(q + 1) * TOK] = np.asarray(res.results[k]["out"])
    return out


def _stage(x, xt, w_inT, wxT, w_outT, wguT, wdT, convw, convb, dparam,
           alog48):
    mask = np.zeros((CH, AUGK), np.float32)
    for tau in range(CH):
        mask[tau, tau:CH] = 536870912.0
    mask[:, CH:] = 536870912.0
    ident16 = np.eye(16, dtype=np.float32)
    in_maps = []
    for k in range(8):
        b, q = k // 4, k % 4
        mh = [1.0 if j < q else 0.0 for j in range(3)]
        mp = [1.0 if (0 < j < q) else 0.0 for j in range(3)]
        cmb = np.tile(np.array(mh + mp + [1.0 - v for v in mp], np.float32),
                      (16, 1))
        in_maps.append({
            "xt": np.ascontiguousarray(xt[b][:, q * TOK: q * TOK + TOK + 3]),
            "w_inT": w_inT, "wxT": wxT, "convw": convw, "convb": convb,
            "dparam": dparam, "alog48": alog48, "w_outT": w_outT,
            "wguT": wguT, "wdT": wdT, "mask": mask, "ident16": ident16,
            "cmb": np.ascontiguousarray(cmb),
        })
    return in_maps


# version 15
# speedup vs baseline: 1.0519x; 1.0518x over previous
import numpy as np
from contextlib import ExitStack
import concourse.bass as bass
import concourse.mybir as mybir
from concourse.bass_utils import run_bass_kernel_spmd

F32 = mybir.dt.float32
BF16 = mybir.dt.bfloat16
AX = mybir.AxisListType.X
ALU = mybir.AluOpType
ACT = mybir.ActivationFunctionType

B, L, DM, DI, DS, FF = 2, 16384, 1024, 2048, 16, 4096
TOK, TT = 4096, 512
NTILE = TOK // TT
CH = 64
NCH = TT // CH
AUGK = CH + DS
DMC, DIC, FFC = DM // 128, DI // 128, FF // 128
C23 = 12582912.0
C52 = 6755399441055744.0
PROBE = 1.5 * 2.0 ** -30
EPS = 1e-5


class Prog:
    ENGINES = ("sync", "vector", "scalar", "tensor", "gpsimd")

    def __init__(self, nc, ctx):
        self.nc, self.ctx = nc, ctx
        self.q = {e: [] for e in self.ENGINES}
        self.sems, self.cnt = {}, {}
        self.w = {e: [] for e in self.ENGINES}
        self.r = {e: [] for e in self.ENGINES}
        self.maxwait = {}
        self.synced = {}

    def sem(self, name):
        if name not in self.sems:
            self.sems[name] = self.ctx.enter_context(self.nc.semaphore(name))
            self.cnt[name] = 0
        return self.sems[name]

    def emit(self, eng, fn):
        self.q[eng].append(fn)

    @staticmethod
    def _foot(ap):
        try:
            t = ap.tensor
            name = t.name
            w = int(t.shape[-1])
            off = int(ap.offset)
            p0, c0 = off // w, off % w
            pext = 0
            cext = 0
            for s, n in ap.ap:
                if n <= 1:
                    continue
                if s > 0 and s % w == 0:
                    pext += (n - 1) * (s // w)
                else:
                    cext += (n - 1) * s
            return (name, p0, p0 + pext, c0, c0 + cext)
        except Exception:
            try:
                return (ap.name, 0, 1 << 30, 0, 1 << 30)
            except Exception:
                return None

    @staticmethod
    def _ovl(a, b):
        return (a[0] == b[0] and a[1] <= b[2] and b[1] <= a[2]
                and a[3] <= b[4] and b[3] <= a[4])

    def _predrain(self, eng, outs, ins):
        ni = [f for f in (self._foot(a) for a in ins) if f]
        no = [f for f in (self._foot(a) for a in outs) if f]
        hit = any(self._ovl(i, w) for i in ni for w in self.w[eng]) or \
            any(self._ovl(o, x) for o in no
                for x in (self.w[eng] + self.r[eng]))
        if hit or len(self.w[eng]) + len(self.r[eng]) > 512:
            self.emit(eng, lambda e: e.drain())
            self.w[eng], self.r[eng] = [], []
        self.w[eng] += no
        self.r[eng] += ni

    def _fin(self, eng, fn, inc):
        if inc is None:
            self.emit(eng, fn)
        else:
            h = self.sem(inc)
            self.cnt[inc] += 1
            self.emit(eng, lambda e: fn(e).then_inc(h, 1))

    def op(self, eng, outs, ins, fn, inc=None):
        self._predrain(eng, outs, ins)
        self._fin(eng, fn, inc)

    def wait(self, eng, sem, val=None):
        h = self.sem(sem)
        v = self.cnt[sem] if val is None else val
        if v > 0:
            self.emit(eng, lambda e: e.wait_ge(h, v))
            self.maxwait[sem] = max(self.maxwait.get(sem, 0), v)
            k = (eng, sem)
            self.synced[k] = max(self.synced.get(k, 0), v)

    def dma(self, eng, out, in_, sem, transpose=False):
        h = self.sem(sem)
        mw = self.maxwait.get(sem, 0)
        if self.synced.get((eng, sem), 0) < mw:
            self.wait(eng, sem, mw)
        self.cnt[sem] += 16
        self.emit(eng, lambda e: e.dma_start(
            out=out, in_=in_, transpose=transpose).then_inc(h, 16))

    def ts(self, eng, out, in0, s1, s2, op0, op1=None, inc=None):
        ins = [in0] + [s for s in (s1, s2) if hasattr(s, "tensor")]
        if op1 is None:
            self.op(eng, [out], ins,
                    lambda e: e.tensor_scalar(out, in0, s1, None, op0), inc)
        else:
            self.op(eng, [out], ins,
                    lambda e: e.tensor_scalar(out, in0, s1, s2, op0, op1), inc)

    def tt(self, eng, out, a, b, alu, inc=None):
        self.op(eng, [out], [a, b], lambda e: e.tensor_tensor(out, a, b, alu), inc)

    def stt(self, eng, out, in0, s, in1, op0, op1, inc=None):
        ins = [in0, in1] + ([s] if hasattr(s, "tensor") else [])
        self.op(eng, [out], ins,
                lambda e: e.scalar_tensor_tensor(out, in0, s, in1, op0, op1), inc)

    def scan(self, eng, out, d0, d1, init, op0, op1, inc=None):
        ins = [d0, d1] + ([init] if hasattr(init, "tensor") else [])
        self.op(eng, [out], ins,
                lambda e: e.tensor_tensor_scan(out, d0, d1, init, op0, op1), inc)

    def red(self, eng, out, in_, alu, absval=False, inc=None, axis=AX):
        self.op(eng, [out], [in_],
                lambda e: e.tensor_reduce(out, in_, axis=axis, op=alu,
                                          apply_absolute_value=absval), inc)

    def cp(self, eng, out, in_, inc=None):
        self.op(eng, [out], [in_], lambda e: e.tensor_copy(out, in_), inc)

    def act(self, eng, out, in_, func, bias=0.0, scale=1.0, accum=None, inc=None):
        ins = [in_] + [a for a in (bias, scale) if hasattr(a, "tensor")]
        outs = [out] + ([accum] if accum is not None else [])
        if accum is None:
            self.op(eng, outs, ins, lambda e: e.activation(
                out, in_, func, bias=bias, scale=scale), inc)
        else:
            self.op(eng, outs, ins, lambda e: e.activation(
                out, in_, func, bias=bias, scale=scale, accum_out=accum), inc)

    def recip(self, eng, out, in_, inc=None):
        self.op(eng, [out], [in_], lambda e: e.reciprocal(out, in_), inc)

    def mset(self, eng, ap, val, inc=None):
        self.op(eng, [ap], [], lambda e: e.memset(ap, val), inc)

    def mm(self, out, lhsT, rhs, start, stop, tp=None, inc=None):
        self._fin("tensor", lambda e: e.matmul(out, lhsT, rhs, start=start,
                                               stop=stop, tile_position=tp), inc)

    def nopinc(self, eng, sem):
        h = self.sem(sem)
        self.cnt[sem] += 1
        if eng == "sync":
            self.emit(eng, lambda e: e.nop().then_inc(h, 1))
        else:
            self.emit(eng, lambda e: e.drain().then_inc(h, 1))
            self.w[eng], self.r[eng] = [], []

    def barrier(self, engines, sem):
        for e in engines:
            self.nopinc(e, sem)
        tgt = self.cnt[sem]
        for e in engines:
            self.wait(e, sem, tgt)

    def flush(self):
        with self.nc.Block() as block:
            for name in self.ENGINES:
                ops = self.q[name]
                if not ops:
                    continue
                def body(e, ops=ops):
                    for f in ops:
                        f(e)
                getattr(block, name)(body)
        self.q = {e: [] for e in self.ENGINES}
        self.w = {e: [] for e in self.ENGINES}
        self.r = {e: [] for e in self.ENGINES}


def build_nc(n_tiles=NTILE, sim_cc=False, dbg=False):
    nc = bass.Bass("TRN2", target_bir_lowering=False, debug=False)
    ctx = ExitStack()

    dram = lambda n, s, d, k="ExternalInput": nc.dram_tensor(n, s, d, kind=k).ap()
    XT = dram("xt", [DM, TOK + 3], F32)
    WIN = dram("w_inT", [DM, 2 * DI], F32)
    WX = dram("wxT", [DI, 33], F32)
    CONVW = dram("convw", [128, 4 * DIC], F32)
    CONVB = dram("convb", [128, DIC], F32)
    DPAR = dram("dparam", [128, DIC], F32)
    ALOG = dram("alog48", [48, 1], F32)
    WOUT = dram("w_outT", [DI, DM], F32)
    WGU = dram("wguT", [DM, 2 * FF], F32)
    WD = dram("wdT", [FF, DM], F32)
    MASK = dram("mask", [CH, AUGK], F32)
    I16 = dram("ident16", [16, 16], F32)
    CMB = dram("cmb", [16, 9], F32)
    OUT = dram("out", [TOK, DM], F32, "ExternalOutput")

    kindd = "ExternalOutput" if dbg else "Internal"
    SP_XC = dram("sp_xc", [DI, TOK], BF16, kindd)
    SP_Z = dram("sp_z", [DI, TOK], BF16, kindd)
    SP_Y = dram("sp_y", [TOK, DI], BF16, kindd)
    SP_RES = dram("sp_res", [DM, TOK], BF16, kindd)
    SP_X1 = dram("sp_x1", [TOK, DM], BF16, kindd)
    SP_AQ = dram("sp_aq", [TOK, DM], BF16, kindd)
    SP_AQ2 = dram("sp_aq2", [TOK, FF], BF16, kindd)
    SP_QGU = dram("sp_qgu", [DM, 2 * FF], BF16, kindd)
    SP_QD = dram("sp_qd", [FF, DM], BF16, kindd)
    CC_IN = dram("cc_in", [16, DI + 1], F32, "Internal")
    CC_OUT = dram("cc_out", [64, DI + 1], F32, "Internal")

    XTr = XT.rearrange("(c p) t -> p c t", p=128)
    WINr = WIN.rearrange("(c p) f -> c p f", p=128)
    WXr = WX.rearrange("(c p) f -> p c f", p=128)
    WOUTr = WOUT.rearrange("(c p) f -> p c f", p=128)
    WGUr = WGU.rearrange("(c p) f -> c p f", p=128)
    WDr = WD.rearrange("(c p) f -> c p f", p=128)
    QGUr = SP_QGU.rearrange("(c p) f -> c p f", p=128)
    QDr = SP_QD.rearrange("(c p) f -> c p f", p=128)
    XCr = SP_XC.rearrange("(c p) t -> p c t", p=128)
    Zr_ = SP_Z.rearrange("(c p) t -> p c t", p=128)
    RESr = SP_RES.rearrange("(c p) t -> p c t", p=128)

    sbuf = lambda n, s, d: ctx.enter_context(nc.sbuf_tensor(n, s, d))
    psum = lambda n, s: ctx.enter_context(nc.psum_tensor(n, s, F32))

    convw_sb = sbuf("convw_sb", [128, 4 * DIC], F32)
    convb_sb = sbuf("convb_sb", [128, DIC], F32)
    dpar_sb = sbuf("dpar_sb", [128, DIC], F32)
    acol = sbuf("acol", [48, 1], F32)
    onecol = sbuf("onecol", [65, 1], F32)
    onesf = sbuf("onesf", [65, 128], F32)
    cprobe = sbuf("cprobe", [128, 1], F32)
    cmag = sbuf("cmag", [128, 1], F32)
    cmagn = sbuf("cmagn", [128, 1], F32)
    epscol = sbuf("epscol", [128, 1], F32)
    nhalf = sbuf("nhalf", [128, 1], F32)
    maskb = sbuf("maskb", [CH, AUGK], F32)
    i16b = sbuf("i16b", [16, 16], BF16)
    cmb_sb = sbuf("cmb_sb", [16, 9], F32)
    wxb = sbuf("wxb", [128, DIC * 33], BF16)
    ppcol = sbuf("ppcol", [16, 1], F32)
    invsa = sbuf("invsa", [128, 32], F32)
    invs2 = sbuf("invs2", [128, 32], F32)
    wsum = sbuf("wsum", [128, 8], F32)
    wmean = sbuf("wmean", [128, 4], F32)
    swcol = sbuf("swcol", [128, 4], F32)
    xmb = sbuf("xmb", [128, 24], BF16)
    stat = sbuf("stat", [128, 16], F32)
    kcol = sbuf("kcol", [128, 8], F32)
    qcol = sbuf("qcol", [128, 3], F32)
    lnst = sbuf("lnst", [128, 48], F32)

    phxs = ExitStack()
    wstg = phxs.enter_context(nc.sbuf_tensor("wstg", [128, 2 * 1024], F32))
    wscr = phxs.enter_context(nc.sbuf_tensor("wscr", [128, 1024], F32))
    wq2 = phxs.enter_context(nc.sbuf_tensor("wq2", [128, 2 * 1024], BF16))

    ph12 = ExitStack()
    s12 = lambda n, s, d: ph12.enter_context(nc.sbuf_tensor(n, s, d))
    cgbuf = s12("cgbuf", [16, TOK], BF16)
    hin = s12("hin", [16, DI], F32)
    hinb = s12("hinb", [16, DI], BF16)
    xbf = s12("xbf", [128, DMC * TT], BF16)

    p_a = psum("p_a", [128, TT])
    p_b = psum("p_b", [128, TT])

    p = Prog(nc, ctx)
    LD, SPL, TR, SS, GP = "s_ld", "s_spl", "s_tr", "s_ss", "s_gp"
    TE, VE, AC, GC = "s_te", "s_ve", "s_ac", "s_gc"
    EVZV, EVZA = "s_evzv", "s_evza"
    BAR, STO, GW = "s_bar", "s_sto", "s_gw"
    SPLZ = ("s_splz0", "s_splz1")
    SSB = "s_ssb"
    for s in (LD, SPL, TR, SS, SSB, GP, TE, VE, AC, GC, EVZV, EVZA, BAR, STO,
              GW) + SPLZ:
        p.sem(s)

    ph1 = ExitStack()
    sb1 = lambda n, s, d: ph1.enter_context(nc.sbuf_tensor(n, s, d))
    ps1 = lambda n, s: ph1.enter_context(nc.psum_tensor(n, s, F32))
    hpack = sb1("hpack", [16, DI + 1], F32)
    p_x = ps1("p_x", [16, TT])
    p_dt2 = ps1("p_dt2", [16, TT])
    p_g = ps1("p_g", [128, TT])
    p_h = ps1("p_h", [128, TT])
    p_m1 = ps1("p_m1", [CH, AUGK])
    p_m2 = ps1("p_m2", [AUGK, AUGK])
    winb = sb1("winb", [128, DMC * 2 * DI], BF16)
    xproj = sb1("xproj", [128, DIC * (TT + 3)], BF16)
    xconv = sb1("xconv", [128, DIC * TT], BF16)
    zsilc = sb1("zsilc", [128, 2 * TT], BF16)
    uw = sb1("uw", [128, 2 * DI], BF16)
    rhsa = sb1("rhsa", [AUGK, 2 * DI], BF16)
    ysb = sb1("ysb", [CH, 2 * DI], BF16)
    dta = sb1("dta", [16, TT], F32)
    s16 = sb1("s16", [16, TT], F32)
    dtsb = sb1("dtsb", [16, TT], F32)
    braw = sb1("braw", [16, TT], F32)
    bbar = sb1("bbar", [16, TT], F32)
    cevac = sb1("cevac", [16, TT], F32)
    fbc = sb1("fbc", [16, TT], F32)
    eargB = sb1("eargB", [16, TT], F32)
    eargC = sb1("eargC", [16, TT], F32)
    bpp = sb1("bpp", [16, TT], BF16)
    cpk = sb1("cpk", [16, TT], BF16)
    ccb = sb1("ccb", [16, 24], F32)
    ecb = sb1("ecb", [16, 24], F32)
    sp8 = sb1("sp8", [16, 8], F32)
    diag2a = sb1("diag2a", [16, 8 * 16], BF16)
    rhsm = sb1("rhsm", [16, NCH * AUGK], BF16)
    taug = sb1("taug", [AUGK, 2 * AUGK], BF16)
    dtch = sb1("dtch", [1, TT], F32)

    p.dma("sync", convw_sb[:], CONVW, LD)
    p.dma("sync", convb_sb[:], CONVB, LD)
    p.dma("sync", dpar_sb[:], DPAR, LD)
    p.dma("sync", acol[:], ALOG, LD)
    p.dma("sync", maskb[:], MASK, LD)
    p.dma("sync", cmb_sb[:], CMB, LD)
    p.dma("sync", hpack[0:16, 0:16], I16, LD)
    for c in range(DMC):
        p.dma("gpsimd", winb[:, c * 4096:(c + 1) * 4096], WINr[c], GW)
    p.dma("gpsimd", wxb[:].rearrange("p (c f) -> p c f", c=DIC), WXr, GW)
    p.dma("gpsimd", xmb[:].rearrange("p (c t) -> p c t", c=8),
          XTr[:, :, 0:3], GW)
    gw_w = p.cnt[GW]

    p.wait("vector", LD)
    p.mset("vector", cprobe[:], PROBE)
    p.ts("vector", cprobe[:], cprobe[:], C23, -C23, ALU.add, ALU.add)
    p.ts("vector", cprobe[:], cprobe[:], 536870912.0, None, ALU.mult)
    p.ts("vector", cmag[:], cprobe[:], -C23, C23, ALU.mult, ALU.add)
    p.stt("vector", cmag[:], cprobe[:], C52, cmag[:], ALU.mult, ALU.add)
    p.ts("vector", cmagn[:], cmag[:], -1.0, None, ALU.mult)
    p.mset("vector", epscol[:], EPS)
    p.mset("vector", qcol[:, 0:1], 1.0)
    p.mset("vector", qcol[:, 1:2], 2.0)
    p.mset("vector", qcol[:, 2:3], -1.0)
    p.mset("vector", nhalf[:], -0.5)
    p.mset("vector", onecol[:], 1.0)
    p.mset("vector", onesf[:], 1.0)
    p.mset("vector", ppcol[:], 1.0)
    p.mset("vector", kcol[:, 0:1], 1.0)
    p.cp("vector", i16b[:], hpack[0:16, 0:16])
    p.mset("vector", hpack[:], 0.0)
    p.mset("vector", rhsa[64:80, 0:DI], 0.0)
    p.nopinc("vector", VE)
    ve_consts = p.cnt[VE]
    p.wait("scalar", LD, None)
    p.act("scalar", acol[:], acol[:], ACT.Exp, inc=AC)
    p.wait("vector", AC, p.cnt[AC])
    p.ts("vector", acol[:], acol[:], -1.0, None, ALU.mult, inc=VE)
    ve_acol = p.cnt[VE]

    wq_chunks = []
    for c in range(DMC):
        for j in range(8):
            col = j * 1024
            wq_chunks.append((WGUr[c][:, col:col + 1024], 1024, j // 4))
    for c in range(FFC):
        wq_chunks.append((WDr[c], 1024, 2))
    wq_marks = {}
    wq_first = [True, True, True]

    def emit_wq_batch(lo, hi):
        for k in range(lo, hi):
            src_ap, width, tgt = wq_chunks[k]
            par = k % 2
            gsl = wstg[:, par * 1024: par * 1024 + width]
            if k >= 2:
                p.wait("gpsimd", VE, wq_marks[k - 2])
            else:
                p.wait("gpsimd", VE, ve_consts)
            p.dma("gpsimd", gsl, src_ap, GP)
            p.wait("vector", GP, None)
            p.red("vector", wsum[:, 4:5], gsl, ALU.add, absval=True)
            if wq_first[tgt]:
                p.cp("vector", wsum[:, tgt:tgt + 1], wsum[:, 4:5], inc=VE)
                wq_first[tgt] = False
            else:
                p.tt("vector", wsum[:, tgt:tgt + 1], wsum[:, tgt:tgt + 1],
                     wsum[:, 4:5], ALU.add, inc=VE)
            wq_marks[k] = p.cnt[VE]

    gu_chunks = []
    for c in range(DMC):
        for j in range(8):
            col = j * 1024
            gu_chunks.append((WGUr[c][:, col:col + 1024],
                              QGUr[c][:, col:col + 1024], j // 4))
    wd_chunks = [(WDr[c], QDr[c], 2) for c in range(FFC)]
    all_chunks = gu_chunks + wd_chunks
    qac_marks = {}

    def _quant_load(ch, base, i):
        k = base + i
        if k - 2 in qac_marks:
            p.wait("gpsimd", AC, qac_marks[k - 2])
        p.dma("gpsimd", wstg[:, (k % 2) * 1024:(k % 2) * 1024 + 1024],
              ch[i][0], GP)

    def emit_quant_tile(ch, base):
        for i in range(min(2, len(ch))):
            _quant_load(ch, base, i)
        for i in range(len(ch)):
            k = base + i
            par = k % 2
            sl = wstg[:, par * 1024:par * 1024 + 1024]
            ql = wq2[:, par * 1024:par * 1024 + 1024]
            p.wait("scalar", GP)
            p.act("scalar", wscr[:], sl, ACT.Identity, bias=cmag[:, 0:1],
                  scale=swcol[:, ch[i][2]:ch[i][2] + 1])
            p.act("scalar", sl, wscr[:], ACT.Identity, bias=cmagn[:, 0:1])
            p.act("scalar", wscr[:], sl, ACT.Relu, bias=qcol[:, 0:1])
            p.act("scalar", sl, wscr[:], ACT.Relu, bias=qcol[:, 1:2],
                  scale=qcol[:, 2:3])
            p.act("scalar", ql, sl, ACT.Identity,
                  bias=qcol[:, 0:1], scale=qcol[:, 2:3], inc=AC)
            qac_marks[k] = p.cnt[AC]
            p.wait("gpsimd", AC, qac_marks[k])
            p.dma("gpsimd", ch[i][1], ql, GP)
            if i + 2 < len(ch):
                _quant_load(ch, base, i + 2)

    ve_wbc = 0
    nwq = min(2, n_tiles)
    pb1 = {"a": None, "b": None, "g": None, "h": None}
    te_aug_end = [0]
    ac_xp = {}
    ve_mgn_x = [0]

    def emit_xz_group(tl, fc):
        T0l = tl * TT
        bk = ("a", "b")[fc % 2]
        bank = (p_a, p_b)[fc % 2]
        if fc == 0:
            p.wait("tensor", GW, None)
        if pb1[bk] is not None:
            p.wait("tensor", *pb1[bk])
        for dm in range(DMC):
            lw = winb[:, dm * 4096 + fc * 128: dm * 4096 + (fc + 1) * 128]
            p.mm(bank[:], lw, xbf[:, dm * TT:(dm + 1) * TT],
                 start=(dm == 0), stop=(dm == DMC - 1),
                 inc=(TE if dm == DMC - 1 else None))
        te_fc = p.cnt[TE]
        if fc < 16:
            if fc == 0:
                p.wait("scalar", VE, ve_mgn_x[0])
            p.wait("scalar", TE, te_fc)
            p.act("scalar", xproj[:, fc * (TT + 3) + 3:
                                  (fc + 1) * (TT + 3)],
                  bank[:], ACT.Identity, inc=EVZA)
            pb1[bk] = (EVZA, p.cnt[EVZA])
            ac_xp[fc] = (EVZA, p.cnt[EVZA])
        else:
            c2 = fc - 16
            pz = c2 % 2
            p.wait("scalar", SPLZ[pz], None)
            p.wait("scalar", TE, te_fc)
            p.act("scalar", zsilc[:, pz * TT:(pz + 1) * TT], bank[:],
                  ACT.Silu, inc=EVZA)
            pb1[bk] = (EVZA, p.cnt[EVZA])
            p.wait("sync", EVZA, None)
            p.dma("sync", Zr_[:, c2:c2 + 1, T0l:T0l + TT],
                  zsilc[:, pz * TT:(pz + 1) * TT]
                  .rearrange("p (c t) -> p c t", c=1), SPLZ[pz])
        return te_fc

    p.dma("gpsimd", xbf[:].rearrange("p (c t) -> p c t", c=DMC),
          XTr[:, :, 3:3 + TT], GW)
    for t in range(n_tiles):
        T0 = t * TT
        ve_x = p.cnt[VE]

        p.wait("tensor", VE, ve_x)
        p.wait("tensor", GW, None)
        for fc in (range(32) if t == 0 else range(16, 32)):
            te_xz = emit_xz_group(t, fc)
        if t + 1 < n_tiles:
            p.wait("gpsimd", TE, te_xz)
            p.dma("gpsimd", xbf[:].rearrange("p (c t) -> p c t", c=DMC),
                  XTr[:, :, 3 + T0 + TT:3 + T0 + 2 * TT], GW)

        if t == 0:
            for bb in ("a", "b"):
                if pb1[bb] is not None:
                    p.wait("tensor", *pb1[bb])
            for fc in range(16):
                for dm in range(DMC):
                    lw = winb[:, dm * 4096 + fc * 128: dm * 4096 + (fc + 1) * 128]
                    p.mm(p_a[:, fc * 3:(fc + 1) * 3], lw,
                         xmb[:, dm * 3:(dm + 1) * 3],
                         start=(fc == 0 and dm == 0),
                         stop=(fc == 15 and dm == DMC - 1))
            p.nopinc("tensor", TE)
            p.wait("vector", TE, p.cnt[TE])
            p.op("vector", [xproj], [p_a], lambda e: e.tensor_copy(
                xproj[:].rearrange("p (c m) -> p c m", c=DIC)[:, :, 0:3],
                p_a[:, 0:48].rearrange("p (c m) -> p c m", c=16)), inc=VE)
            pb1["a"] = (VE, p.cnt[VE])
        ve_mgn = p.cnt[VE]

        p.wait("vector", VE, ve_mgn)
        ve_conv = {}
        for k in (0, 1, 2, 3):
            for dc in range(DIC):
                base = dc * (TT + 3)
                dst = xconv[:, dc * TT:(dc + 1) * TT]
                if k == 0:
                    p.wait("vector", *ac_xp[dc])
                    p.ts("vector", dst, xproj[:, base:base + TT],
                         convw_sb[:, dc * 4:dc * 4 + 1], convb_sb[:, dc:dc + 1],
                         ALU.mult, ALU.add)
                else:
                    p.stt("vector", dst, xproj[:, base + k:base + k + TT],
                          convw_sb[:, dc * 4 + k:dc * 4 + k + 1], dst,
                          ALU.mult, ALU.add, inc=(VE if k == 3 else None))
                if k == 3:
                    ve_conv[dc] = p.cnt[VE]
        ac_sil = {}
        for dc in range(DIC):
            p.wait("scalar", VE, ve_conv[dc])
            sl = xconv[:, dc * TT:(dc + 1) * TT]
            p.act("scalar", sl, sl, ACT.Silu, inc=AC)
            ac_sil[dc] = p.cnt[AC]
        ac_conv = p.cnt[AC]

        p.wait("sync", AC, ac_conv)
        p.dma("sync", XCr[:, :, T0:T0 + TT],
              xconv[:].rearrange("p (c t) -> p c t", c=DIC), SPL)
        spl_xc = p.cnt[SPL]

        if t == nwq and ve_wbc:
            p.wait("tensor", VE, ve_wbc)
        for dc in range(DIC):
            p.wait("tensor", AC, ac_sil[dc])
            rhs = xconv[:, dc * TT:(dc + 1) * TT]
            p.mm(p_x[:], wxb[:, dc * 33:dc * 33 + 16], rhs,
                 start=(dc == 0), stop=(dc == DIC - 1),
                 inc=(TE if dc == DIC - 1 else None))
        te_B = p.cnt[TE]
        for dc in range(DIC):
            rhs = xconv[:, dc * TT:(dc + 1) * TT]
            p.mm(p_dt2[0:1, :], wxb[:, dc * 33 + 32:dc * 33 + 33], rhs,
                 start=(dc == 0), stop=(dc == DIC - 1),
                 inc=(TE if dc == DIC - 1 else None))
        te_dt = p.cnt[TE]

        p.wait("vector", TE, te_B)
        p.cp("vector", braw[:], p_x[:], inc=VE)
        ve_braw = p.cnt[VE]
        p.wait("vector", TE, te_dt)
        p.ts("vector", dtch[0:1, :], p_dt2[0:1, :], 0.0, None, ALU.min,
             inc=VE)
        p.wait("scalar", VE, p.cnt[VE])
        p.act("scalar", dtch[0:1, :], dtch[0:1, :], ACT.Exp)
        p.act("scalar", dtch[0:1, :], dtch[0:1, :], ACT.Ln,
              bias=onecol[0:1, 0:1], inc=AC)
        p.wait("vector", AC, p.cnt[AC])
        p.ts("vector", dtch[0:1, :], dtch[0:1, :], 0.1, 0.001,
             ALU.min, ALU.max, inc=VE)
        ve_dtc = p.cnt[VE]

        p.wait("tensor", VE, ve_braw)
        for dc in range(DIC):
            rhs = xconv[:, dc * TT:(dc + 1) * TT]
            p.mm(p_x[:], wxb[:, dc * 33 + 16:dc * 33 + 32], rhs,
                 start=(dc == 0), stop=(dc == DIC - 1),
                 inc=(TE if dc == DIC - 1 else None))
        te_C = p.cnt[TE]
        p.wait("tensor", VE, ve_dtc)
        p.mm(p_dt2[:], onesf[0:1, 0:16], dtch[0:1, :], True, True, inc=TE)
        te_bc = p.cnt[TE]

        p.wait("vector", TE, te_bc)
        if t == 0:
            p.wait("vector", VE, ve_acol)
        p.cp("vector", dtsb[:], p_dt2[:])
        p.ts("vector", dta[:], p_dt2[:], acol[0:16, 0:1], -20.0,
             ALU.mult, ALU.max)
        p.tt("vector", bbar[:], braw[:], dtsb[:], ALU.mult)
        p.ts("vector", bbar[:], bbar[:], 10.0, -10.0, ALU.min, ALU.max)
        p.wait("vector", TE, te_C)
        p.cp("vector", cevac[:], p_x[:])
        p.scan("vector", s16[:], dta[:], dta[:], 0.0, ALU.add, ALU.bypass)
        p.op("vector", [fbc], [s16], lambda e: e.tensor_copy(
            fbc[:].rearrange("p (c t) -> p c t", c=NCH),
            s16[:, 31::64].unsqueeze(2).broadcast_to((16, NCH, CH))))
        p.mset("vector", sp8[:, 0:1], 0.0)
        p.op("vector", [sp8], [s16], lambda e: e.tensor_copy(
            sp8[:, 1:8], s16[:, 63::64][:, 0:7]))
        p.tt("vector", ccb[:, 0:8], s16[:, 63::64], s16[:, 31::64],
             ALU.subtract)
        p.tt("vector", ccb[:, 8:16], s16[:, 31::64], sp8[:], ALU.subtract)
        p.tt("vector", ccb[:, 16:24], s16[:, 63::64], sp8[:], ALU.subtract)
        p.tt("vector", eargB[:], fbc[:], s16[:], ALU.subtract)
        p.ts("vector", eargB[:], eargB[:], 87.0, -20.10126823437577,
             ALU.min, ALU.add)
        p.tt("vector", eargC[:], s16[:], fbc[:], ALU.subtract)
        p.ts("vector", eargC[:], eargC[:], 87.0, None, ALU.min)
        p.nopinc("vector", VE)
        p.wait("scalar", VE, p.cnt[VE])
        p.act("scalar", ecb[:], ccb[:], ACT.Exp)
        p.act("scalar", eargB[:], eargB[:], ACT.Exp)
        p.act("scalar", eargC[:], eargC[:], ACT.Exp, inc=AC)
        p.wait("vector", AC, p.cnt[AC])
        p.tt("vector", bpp[:], bbar[:], eargB[:], ALU.mult)
        p.tt("vector", cpk[:], cevac[:], eargC[:], ALU.mult)
        p.op("vector", [rhsm], [cpk], lambda e: e.tensor_copy(
            rhsm[:].rearrange("p (c k) -> p c k", k=AUGK)[:, :, 0:CH],
            cpk[:].rearrange("p (c t) -> p c t", c=NCH)))
        for c in range(NCH):
            p.ts("vector", rhsm[:, c * AUGK + CH:(c + 1) * AUGK], i16b[:],
                 ecb[:, c:c + 1], None, ALU.mult)
            p.ts("vector", diag2a[:, c * 16:(c + 1) * 16], i16b[:],
                 ecb[:, 8 + c:9 + c], None, ALU.mult)
        for c in range(NCH):
            p.tt("vector", stat[0:16, 8:9], ecb[:, 8 + c:9 + c], ppcol[:],
                 ALU.mult)
            p.ts("vector", cgbuf[:, T0 + c * CH:T0 + (c + 1) * CH],
                 cpk[:, c * CH:(c + 1) * CH], stat[0:16, 8:9], None, ALU.mult)
            p.ts("vector", ppcol[:], ppcol[:], ecb[:, 16 + c:17 + c],
                 None, ALU.mult)
        p.nopinc("vector", VE)
        ve_prep = p.cnt[VE]

        if t + 1 < n_tiles:
            p.op("vector", [xproj[:]], [xproj[:]], lambda e: e.tensor_copy(
                xproj[:].rearrange("p (c m) -> p c m", c=DIC)[:, :, 0:3],
                xproj[:].rearrange("p (c m) -> p c m", c=DIC)[:, :, TT:TT + 3]))
            p.nopinc("vector", VE)
            ve_mgn_x[0] = p.cnt[VE]
        if n_tiles == NTILE and 2 <= t:
            p.wait("scalar", VE, ve_wbc)
            p.wait("gpsimd", VE, ve_wbc)

        def issue_reload(w):
            p.wait("sync", SPL, None)
            p.wait("sync", SS, None)
            p.wait("sync", SSB, None)
            p.dma("sync", uw[:, (w % 2) * DI:(w % 2) * DI + DI],
                  SP_XC[:, T0 + w * 128:T0 + w * 128 + 128], TR,
                  transpose=True)

        def issue_copy(c):
            gc_ = t * NCH + c
            pr = gc_ % 2
            p.wait("sync", TR, None)
            if te_aug_end[0]:
                p.wait("sync", TE, te_aug_end[0])
            p.dma("sync", rhsa[0:64, pr * DI:(pr + 1) * DI],
                  uw[(c % 2) * 64:(c % 2) * 64 + 64,
                     ((c // 2) % 2) * DI:((c // 2) % 2) * DI + DI],
                  SS if pr == 0 else SSB)

        def emit_mb(c, first):
            if first:
                p.wait("tensor", VE, ve_prep)
            else:
                p.wait("tensor", VE, ve_taug[c - 1])
            p.mm(p_m1[:, :], bpp[:, c * CH:(c + 1) * CH],
                 rhsm[:, c * AUGK:(c + 1) * AUGK], True, True, tp=(0, 0))
            p.mm(p_m2[64:80, :], diag2a[:, c * 16:(c + 1) * 16],
                 rhsm[:, c * AUGK:(c + 1) * AUGK], True, True, tp=(0, 64),
                 inc=TE)
            te_mb[c] = p.cnt[TE]

        ve_taug = {}
        te_mb = {}
        emit_mb(0, True)
        issue_reload(0)
        issue_copy(0)
        issue_reload(1)
        for c in range(NCH):
            w0 = c * CH
            gci = t * NCH + c
            par, npar = gci % 2, (gci + 1) % 2

            tb = (c % 2) * AUGK
            p.wait("vector", TE, te_mb[c])
            p.tt("vector", taug[0:64, tb:tb + AUGK], p_m1[:, :], maskb[:],
                 ALU.mult)
            p.cp("vector", taug[64:80, tb:tb + AUGK], p_m2[64:80, :], inc=VE)
            ve_taug[c] = p.cnt[VE]

            if c + 1 < NCH:
                issue_copy(c + 1)
            if c % 2 == 1 and (c + 3) // 2 < NCH // 2:
                issue_reload((c + 3) // 2)
            if c + 1 < NCH:
                emit_mb(c + 1, False)

            p.wait("tensor", VE, ve_taug[c])
            p.wait("tensor", SS if par == 0 else SSB, None)
            p.wait("vector", SPL, None)
            yb = par * DI
            for n in range(4):
                bk = ("g", "h")[n % 2]
                bank = (p_g, p_h)[n % 2]
                if pb1[bk] is not None:
                    p.wait("tensor", *pb1[bk])
                p.mm(bank[0:80, :], taug[:, tb:tb + AUGK],
                     rhsa[:, par * DI + n * 512:par * DI + (n + 1) * 512],
                     True, True, inc=TE)
                p.wait("vector", TE, p.cnt[TE])
                p.ts("vector", rhsa[64:80, npar * DI + n * TT:
                                    npar * DI + (n + 1) * TT],
                     bank[64:80, :], 100.0, -100.0, ALU.min, ALU.max)
                p.cp("vector", ysb[:, yb + n * TT:yb + (n + 1) * TT],
                     bank[0:64, :], inc=VE)
                pb1[bk] = (VE, p.cnt[VE])
                if n == 1 and t + 1 < n_tiles:
                    emit_xz_group(t + 1, 2 * c)
                if n == 2 and t + 1 < n_tiles:
                    emit_xz_group(t + 1, 2 * c + 1)
            te_aug_end[0] = p.cnt[TE]
            p.wait("sync", VE, None)
            p.dma("sync", SP_Y[T0 + w0:T0 + w0 + CH, :],
                  ysb[:, yb:yb + DI], SPL)
            if n_tiles == NTILE and 2 <= t:
                k = 8 * (t - 2) + c
                emit_quant_tile(all_chunks[k:k + 1], k)

        per = (96 + nwq - 1) // nwq
        if t < nwq:
            emit_wq_batch(t * per, min((t + 1) * per, 96))
        if t == nwq - 1:
            p.wait("vector", TE, None)
            p.cp("vector", kcol[:, 1:4], wsum[:, 0:3], inc=VE)
            p.wait("tensor", VE, None)
            if pb1["a"] is not None:
                p.wait("tensor", *pb1["a"])
            p.mm(p_a[0:1, 0:3], kcol[:, 0:1], kcol[:, 1:4], True, True,
                 inc=TE)
            p.wait("vector", TE, None)
            p.cp("vector", stat[0:1, 4:7], p_a[0:1, 0:3], inc=VE)
            p.wait("tensor", VE, None)
            p.mm(p_a[0:128, 4:7], onesf[0:1, 0:128], stat[0:1, 4:7],
                 True, True, inc=TE)
            p.wait("vector", TE, None)
            p.cp("vector", wsum[:, 5:8], p_a[0:128, 4:7], inc=VE)
            pb1["a"] = (VE, p.cnt[VE])
            p.ts("vector", wmean[:, 0:3], wsum[:, 5:8], 1.0 / (FF * DM),
                 EPS, ALU.mult, ALU.max)
            p.recip("vector", swcol[:, 0:1], wmean[:, 0:1])
            p.recip("vector", swcol[:, 1:2], wmean[:, 1:2])
            p.recip("vector", swcol[:, 2:3], wmean[:, 2:3], inc=VE)
            ve_wbc = p.cnt[VE]

    par_end = (NTILE * NCH) % 2
    p.wait("vector", VE, None)
    p.cp("vector", hpack[:, 0:DI],
         rhsa[64:80, par_end * DI:(par_end + 1) * DI])
    p.cp("vector", hpack[:, DI:DI + 1], ppcol[:], inc=VE)
    p.wait("sync", VE, None)
    p.dma("sync", CC_IN, hpack[:], SPL)
    cc_ld = p.cnt[SPL]

    p.wait("sync", SPL, None)
    p.wait("sync", SPLZ[0], None)
    p.wait("sync", SPLZ[1], None)
    p.wait("sync", SS, None)
    p.wait("sync", TR, None)
    p.wait("sync", LD, None)

    p.wait("gpsimd", SPL, cc_ld)
    if sim_cc:
        for j in range(4):
            p.dma("gpsimd", CC_OUT[j * 16:(j + 1) * 16, :], CC_IN, GP)
        p.wait("gpsimd", GP, None)
        p.nopinc("gpsimd", GC)
    else:
        p.op("gpsimd", [], [], lambda e: e.collective_compute(
            "AllGather", ALU.bypass,
            replica_groups=[[0, 1, 2, 3], [4, 5, 6, 7]],
            ins=[CC_IN], outs=[CC_OUT]), inc=GC)
    gc_cc = p.cnt[GC]

    p.barrier(("sync", "vector", "scalar", "tensor"), BAR)
    p.flush()
    ph1.close()

    phx = ExitStack()
    p_c = phx.enter_context(nc.psum_tensor("p_c", [128, TT], F32))
    p_d = phx.enter_context(nc.psum_tensor("p_d", [128, TT], F32))
    PBANKS = (p_a, p_b, p_c, p_d)
    pb_marks = [None, None, None, None]


    ph2 = ExitStack()
    sb2 = lambda n, s, d: ph2.enter_context(nc.sbuf_tensor(n, s, d))
    woutb = sb2("woutb", [128, DIC * DM], BF16)
    yfm = sb2("yfm", [128, 2 * DIC * TT], BF16)
    xcr = sb2("xcr", [128, DIC * TT], BF16)
    zr = sb2("zr", [128, DIC * TT], BF16)
    ygate = sb2("ygate", [128, DIC * TT], BF16)
    resb = sb2("resb", [128, DMC * TT], BF16)
    rest4 = sb2("rest4", [128, 4 * DM], BF16)
    x1t = sb2("x1t", [128, 4 * DM], BF16)
    aqt = sb2("aqt", [128, 4 * DM], BF16)
    sqt = sb2("sqt", [128, DM], F32)
    gsh = sb2("gsh", [16, DI + 1], F32)

    p.wait("sync", GC, gc_cc)
    p.mset("vector", hin[:], 0.0)
    for j in range(3):
        p.wait("sync", VE, None)
        p.dma("sync", gsh[:], CC_OUT[j * 16:(j + 1) * 16, :], LD)
        p.wait("vector", LD)
        p.ts("vector", stat[0:16, 9:10], gsh[:, DI:DI + 1],
             cmb_sb[:, 3 + j:4 + j], None, ALU.mult)
        p.tt("vector", stat[0:16, 9:10], stat[0:16, 9:10],
             cmb_sb[:, 6 + j:7 + j], ALU.add)
        p.ts("vector", gsh[:, 0:DI], gsh[:, 0:DI], cmb_sb[:, j:j + 1],
             None, ALU.mult)
        p.stt("vector", hin[:], hin[:], stat[0:16, 9:10], gsh[:, 0:DI],
              ALU.mult, ALU.add)
        p.nopinc("vector", VE)
    p.ts("vector", hin[:], hin[:], 100.0, -100.0, ALU.min, ALU.max)
    p.cp("vector", hinb[:], hin[:], inc=VE)
    ve_hin = p.cnt[VE]

    p.dma("gpsimd", woutb[:].rearrange("p (c f) -> p c f", c=DIC), WOUTr, GW)
    gw_wo = p.cnt[GW]

    p.wait("gpsimd", VE, None)
    p.dma("gpsimd", xbf[:].rearrange("p (c t) -> p c t", c=DMC),
          XTr[:, :, 3:3 + TT], GW)

    def b2_loads(tl, ve_guard):
        T0l = tl * TT
        pb = (tl % 2) * DIC * TT
        if ve_guard:
            p.wait("sync", VE, ve_guard)
        p.dma("sync", xcr[:].rearrange("p (c t) -> p c t", c=DIC),
              XCr[:, :, T0l:T0l + TT], LD)
        p.dma("sync", zr[:].rearrange("p (c t) -> p c t", c=DIC),
              Zr_[:, :, T0l:T0l + TT], LD)
        for dc in range(DIC):
            p.dma("sync", yfm[:, pb + dc * TT:pb + (dc + 1) * TT],
                  SP_Y[T0l:T0l + TT, dc * 128:(dc + 1) * 128], TR,
                  transpose=True)

    b2_loads(0, 0)
    for t in range(n_tiles):
        T0 = t * TT
        pb = (t % 2) * DIC * TT
        p.wait("vector", LD)
        p.wait("vector", TR)
        p.wait("vector", GW)
        p.nopinc("vector", VE)
        ve_rel = p.cnt[VE]

        p.wait("tensor", VE, ve_rel)
        if t == 0:
            p.wait("tensor", VE, ve_hin)
        for dc in range(DIC):
            bi = dc % 4
            bank = PBANKS[bi]
            if pb_marks[bi] is not None:
                p.wait("tensor", *pb_marks[bi])
            p.mm(bank[:], hinb[:, dc * 128:(dc + 1) * 128],
                 cgbuf[:, T0:T0 + TT], True, True, inc=TE)
            p.wait("vector", TE, p.cnt[TE])
            sl = slice(dc * TT, (dc + 1) * TT)
            p.tt("vector", ygate[:, sl], yfm[:, pb + dc * TT:pb + (dc + 1) * TT],
                 bank[:], ALU.add, inc=VE)
            pb_marks[bi] = (VE, p.cnt[VE])
        p.tt("vector", ygate[:], ygate[:], zr[:], ALU.mult)
        for dc in range(DIC):
            sl = slice(dc * TT, (dc + 1) * TT)
            p.stt("vector", ygate[:, sl], xcr[:, sl],
                  dpar_sb[:, dc:dc + 1], ygate[:, sl], ALU.mult, ALU.add)
        p.nopinc("vector", VE)

        ve_gate = p.cnt[VE]
        p.wait("tensor", VE, ve_gate)
        if t == 0:
            p.wait("tensor", GW, None)
        for dmc in range(DMC):
            bi = dmc % 4
            bank = PBANKS[bi]
            if pb_marks[bi] is not None:
                p.wait("tensor", *pb_marks[bi])
            for dc in range(DIC):
                lw = woutb[:, dc * 1024 + dmc * 128: dc * 1024 + (dmc + 1) * 128]
                p.mm(bank[:], lw, ygate[:, dc * TT:(dc + 1) * TT],
                     start=(dc == 0), stop=(dc == DIC - 1),
                     inc=(TE if dc == DIC - 1 else None))
            p.wait("vector", TE, p.cnt[TE])
            p.tt("vector", resb[:, dmc * TT:(dmc + 1) * TT], bank[:],
                 xbf[:, dmc * TT:(dmc + 1) * TT], ALU.add, inc=VE)
            pb_marks[bi] = (VE, p.cnt[VE])
        ve_res = p.cnt[VE]
        p.wait("sync", VE, None)
        p.dma("sync", RESr[:, :, T0:T0 + TT],
              resb[:].rearrange("p (c t) -> p c t", c=DMC), SPL)
        spl_res = p.cnt[SPL]
        if t + 1 < n_tiles:
            b2_loads(t + 1, ve_gate)

        p.wait("sync", SPL, spl_res)
        p.wait("sync", VE, None)
        for w in range(4):
            p.dma("sync", rest4[:, w * DM:(w + 1) * DM],
                  SP_RES[:, T0 + w * 128:T0 + w * 128 + 128],
                  TR, transpose=True)
        p.wait("vector", TR)
        for w in range(4):
            p.red("vector", lnst[:, w:w + 1], rest4[:, w * DM:(w + 1) * DM],
                  ALU.add)
        p.nopinc("vector", VE)
        p.wait("scalar", VE, p.cnt[VE])
        for w in range(4):
            p.act("scalar", sqt[:], rest4[:, w * DM:(w + 1) * DM], ACT.Square,
                  accum=lnst[:, 4 + w:5 + w], inc=AC)
        p.wait("vector", AC, p.cnt[AC])
        p.ts("vector", lnst[:, 8:12], lnst[:, 0:4], 1.0 / DM, None, ALU.mult)
        p.ts("vector", lnst[:, 12:16], lnst[:, 4:8], 1.0 / DM, None, ALU.mult)
        p.tt("vector", lnst[:, 16:20], lnst[:, 8:12], lnst[:, 8:12], ALU.mult)
        p.tt("vector", lnst[:, 20:24], lnst[:, 12:16], lnst[:, 16:20],
             ALU.subtract, inc=VE)
        p.wait("scalar", VE, p.cnt[VE])
        p.act("scalar", lnst[:, 24:28], lnst[:, 20:24], ACT.Ln,
              bias=epscol[:, 0:1])
        p.act("scalar", lnst[:, 28:32], lnst[:, 24:28], ACT.Exp,
              scale=nhalf[:, 0:1], inc=AC)
        p.wait("vector", AC, p.cnt[AC])
        p.wait("vector", SPL, None)
        for w in range(4):
            p.ts("vector", x1t[:, w * DM:(w + 1) * DM],
                 rest4[:, w * DM:(w + 1) * DM], lnst[:, 8 + w:9 + w],
                 lnst[:, 28 + w:29 + w], ALU.subtract, ALU.mult)
        for w in range(4):
            p.red("vector", lnst[:, 32 + w:33 + w],
                  x1t[:, w * DM:(w + 1) * DM], ALU.max, absval=True)
        p.ts("vector", lnst[:, 36:40], lnst[:, 32:36], EPS, None, ALU.max)
        p.recip("vector", lnst[:, 40:44], lnst[:, 36:40])
        p.ts("vector", lnst[:, 44:48], lnst[:, 40:44], 127.0, None, ALU.mult)
        p.ts("vector", invsa[:, t * 4:t * 4 + 4], lnst[:, 36:40],
             1.0 / 127.0, None, ALU.mult)
        p.nopinc("vector", VE)
        ve_ln = p.cnt[VE]
        p.wait("scalar", VE, ve_ln)
        p.wait("scalar", SPL, None)
        for w in range(4):
            p.act("scalar", sqt[:], x1t[:, w * DM:(w + 1) * DM], ACT.Identity,
                  bias=cmag[:, 0:1], scale=lnst[:, 44 + w:45 + w])
            p.act("scalar", aqt[:, w * DM:(w + 1) * DM], sqt[:], ACT.Identity,
                  bias=cmagn[:, 0:1], inc=(AC if w == 3 else None))
        p.wait("sync", VE, None)
        p.wait("sync", AC, None)
        p.dma("sync", SP_X1[T0:T0 + TT, :].rearrange("(w p) f -> p w f", p=128),
              x1t[:].rearrange("p (w f) -> p w f", w=4), SPL)
        p.dma("sync", SP_AQ[T0:T0 + TT, :].rearrange("(w p) f -> p w f", p=128),
              aqt[:].rearrange("p (w f) -> p w f", w=4), SPL)

        if t + 1 < n_tiles:
            p.wait("gpsimd", VE, ve_res)
            p.dma("gpsimd", xbf[:].rearrange("p (c t) -> p c t", c=DMC),
                  XTr[:, :, 3 + T0 + TT:3 + T0 + 2 * TT], GW)
        b1q = 48 if n_tiles == NTILE else 0
        pg = (96 - b1q + n_tiles - 1) // n_tiles
        emit_quant_tile(all_chunks[b1q + pg * t:
                                   min(b1q + pg * (t + 1), 96)], b1q + pg * t)

    p.barrier(("sync", "vector", "scalar", "tensor"), BAR)
    p.flush()
    ph2.close()
    ph12.close()
    phxs.close()

    ph3 = ExitStack()
    sb3 = lambda n, s, d: ph3.enter_context(nc.sbuf_tensor(n, s, d))
    wgub = sb3("wgub", [128, DMC * 2 * FF], BF16)
    aqf = sb3("aqf", [128, 2 * DMC * TT], BF16)
    rgb = sb3("rgb", [128, FF], BF16)
    ub = sb3("ub", [128, FF], BF16)
    hb = sb3("hb", [128, 2 * FF], BF16)
    hqp = sb3("hqp", [128, TT], F32)

    p.wait("sync", GP, None)
    p.wait("sync", SPL, None)
    ve_rgb_free = 0
    ve_ub_free = 0
    p.wait("sync", TR, None)
    p.wait("sync", LD, None)
    p.wait("sync", GW, None)
    for c in range(DMC):
        p.dma("sync", wgub[:, c * 2 * FF:(c + 1) * 2 * FF], QGUr[c], LD)
    ld_gu = p.cnt[LD]
    p.wait("tensor", LD, ld_gu)

    p.tt("vector", lnst[:, 40:41], wmean[:, 0:1], wmean[:, 0:1], ALU.mult)
    p.tt("vector", lnst[:, 40:41], lnst[:, 40:41], wmean[:, 1:2], ALU.mult)
    nsb = 4 * n_tiles
    p.tt("vector", lnst[:, 0:nsb], invsa[:, 0:nsb], invsa[:, 0:nsb], ALU.mult)
    p.tt("vector", lnst[:, 0:nsb], lnst[:, 0:nsb], invsa[:, 0:nsb], ALU.mult)
    p.ts("vector", lnst[:, 0:nsb], lnst[:, 0:nsb], lnst[:, 40:41], None,
         ALU.mult)

    pend = None

    def q2_piece(grp):
        pT0, ps0, ppar, pve = pend
        sl = slice(ppar * FF + grp * TT, ppar * FF + (grp + 1) * TT)
        if grp == 0:
            p.wait("scalar", VE, pve)
        p.act("scalar", hqp[:], hb[:, sl], ACT.Identity, bias=cmag[:, 0:1],
              scale=kcol[:, 6 + ppar:7 + ppar])
        p.act("scalar", hb[:, sl], hqp[:], ACT.Identity, bias=cmagn[:, 0:1],
              inc=(AC if grp == 7 else None))
        if grp == 7:
            p.wait("sync", AC, p.cnt[AC])
            p.dma("sync", SP_AQ2[pT0 + ps0:pT0 + ps0 + 128, :],
                  hb[:, ppar * FF:(ppar + 1) * FF], SPL)

    def b3_aq(tl):
        T0l = tl * TT
        ab = (tl % 2) * DMC * TT
        for dm in range(DMC):
            p.dma("sync", aqf[:, ab + dm * TT:ab + (dm + 1) * TT],
                  SP_AQ[T0l:T0l + TT, dm * 128:(dm + 1) * 128], TR,
                  transpose=True)

    sidx = 0
    te_end = [0, 0]
    b3_aq(0)
    for t in range(n_tiles):
        T0 = t * TT
        ab = (t % 2) * DMC * TT
        p.wait("tensor", TR, None)
        for sub in range(4):
            s0 = sub * 128
            par = sidx % 2
            hbs = hb[:, par * FF:(par + 1) * FF]
            for grp in range(8):
                bis = ((grp % 2) * 2, (grp % 2) * 2 + 1)
                for n in range(2):
                    if pb_marks[bis[n]] is not None:
                        p.wait("tensor", *pb_marks[bis[n]])
                for dm in range(DMC):
                    lw = aqf[:, ab + dm * TT + s0:ab + dm * TT + s0 + 128]
                    for n in range(2):
                        f = grp * 2 + n
                        bank = PBANKS[bis[n]]
                        p.mm(bank[:], lw,
                             wgub[:, dm * 2 * FF + f * TT:
                                  dm * 2 * FF + (f + 1) * TT],
                             start=(dm == 0), stop=(dm == DMC - 1),
                             inc=(TE if (dm == DMC - 1 and n == 1) else None))
                te_g = p.cnt[TE]
                for n in range(2):
                    f = grp * 2 + n
                    bank = PBANKS[bis[n]]
                    if f < 8:
                        p.wait("scalar", TE, te_g)
                        p.wait("scalar", VE, ve_rgb_free)
                        p.act("scalar", rgb[:, f * TT:(f + 1) * TT], bank[:],
                              ACT.Relu, inc=AC)
                        pb_marks[bis[n]] = (AC, p.cnt[AC])
                    else:
                        p.wait("vector", TE, te_g)
                        p.wait("vector", VE, ve_ub_free)
                        p.cp("vector", ub[:, (f - 8) * TT:(f - 7) * TT],
                             bank[:], inc=VE)
                        pb_marks[bis[n]] = (VE, p.cnt[VE])
                if pend is not None:
                    q2_piece(grp)
            p.wait("vector", AC, None)
            p.wait("vector", SPL, None)
            p.tt("vector", hbs, rgb[:], rgb[:], ALU.mult, inc=VE)
            ve_rgb_free = p.cnt[VE]
            p.tt("vector", hbs, hbs, ub[:], ALU.mult, inc=VE)
            ve_ub_free = p.cnt[VE]
            p.red("vector", kcol[:, 0:1], hbs, ALU.max, absval=True)
            k1 = lnst[:, t * 4 + sub:t * 4 + sub + 1]
            p.tt("vector", kcol[:, 3:4], kcol[:, 0:1], k1, ALU.mult)
            p.ts("vector", kcol[:, 3:4], kcol[:, 3:4], EPS, None, ALU.max)
            p.recip("vector", kcol[:, 4:5], kcol[:, 3:4])
            p.tt("vector", kcol[:, 5:6], kcol[:, 4:5], k1, ALU.mult)
            p.ts("vector", kcol[:, 6 + par:7 + par], kcol[:, 5:6], 127.0,
                 None, ALU.mult, inc=VE)
            ve_k = p.cnt[VE]
            p.ts("vector", invs2[:, t * 4 + sub:t * 4 + sub + 1], kcol[:, 3:4],
                 1.0 / 127.0, None, ALU.mult, inc=VE)
            pend = (T0, s0, par, ve_k)
            sidx += 1
        if t + 1 < n_tiles:
            p.wait("sync", TE, te_end[(t + 1) % 2])
            b3_aq(t + 1)
        te_end[t % 2] = p.cnt[TE]

    for grp in range(8):
        q2_piece(grp)
    pend = None

    p.barrier(("sync", "vector", "scalar", "tensor"), BAR)
    p.flush()
    ph3.close()

    ph4 = ExitStack()
    sb4 = lambda n, s, d: ph4.enter_context(nc.sbuf_tensor(n, s, d))
    wdb = sb4("wdb", [128, FFC * DM], BF16)
    aq2f = sb4("aq2f", [128, 2 * FFC * TT], BF16)
    x1r = sb4("x1r", [128, 2 * 4 * DM], BF16)
    dnb4 = sb4("dnb4", [128, 4 * DM], F32)
    ost = sb4("ost", [128, 4 * DM], F32)

    p.wait("sync", SPL, None)
    p.wait("sync", TR, None)
    p.wait("sync", LD, None)
    p.wait("sync", GP, None)
    for c in range(FFC):
        p.dma("sync", wdb[:, c * DM:(c + 1) * DM], QDr[c], LD)
    ld_d = p.cnt[LD]
    p.wait("tensor", LD, ld_d)

    def b4_loads(tl, ve_guard, te_guard):
        T0l = tl * TT
        ab = (tl % 2) * FFC * TT
        xb = (tl % 2) * 4 * DM
        if ve_guard:
            p.wait("sync", VE, ve_guard)
        p.dma("sync", x1r[:, xb:xb + 4 * DM].rearrange("p (w f) -> p w f", w=4),
              SP_X1[T0l:T0l + TT, :].rearrange("(w p) f -> p w f", p=128), LD)
        if te_guard:
            p.wait("sync", TE, te_guard)
        for ic in range(FFC):
            p.dma("sync", aq2f[:, ab + ic * TT:ab + (ic + 1) * TT],
                  SP_AQ2[T0l:T0l + TT, ic * 128:(ic + 1) * 128],
                  TR, transpose=True)

    te_end = [0, 0]
    ve_x1 = [0, 0]
    b4_loads(0, 0, 0)
    for t in range(n_tiles):
        T0 = t * TT
        ab = (t % 2) * FFC * TT
        xb = (t % 2) * 4 * DM
        p.wait("tensor", TR, None)
        for sub in range(4):
            s0 = sub * 128
            bis = ((sub % 2) * 2, (sub % 2) * 2 + 1)
            for n in range(2):
                if pb_marks[bis[n]] is not None:
                    p.wait("tensor", *pb_marks[bis[n]])
            for ic in range(FFC):
                lw = aq2f[:, ab + ic * TT + s0:ab + ic * TT + s0 + 128]
                for n in range(2):
                    bank = PBANKS[bis[n]]
                    p.mm(bank[:], lw,
                         wdb[:, ic * DM + n * TT:ic * DM + (n + 1) * TT],
                         start=(ic == 0), stop=(ic == FFC - 1),
                         inc=(TE if (ic == FFC - 1 and n == 1) else None))
            te_d = p.cnt[TE]
            p.wait("vector", TE, te_d)
            if sub == 0:
                p.wait("vector", LD, None)
                p.wait("vector", STO, None)
            iv = invs2[:, t * 4 + sub:t * 4 + sub + 1]
            wsl = slice(sub * DM, (sub + 1) * DM)
            for n in range(2):
                bank = PBANKS[bis[n]]
                p.ts("vector", dnb4[:, sub * DM + n * TT:
                                sub * DM + (n + 1) * TT], bank[:], iv,
                     wmean[:, 2:3], ALU.mult, ALU.mult, inc=VE)
                pb_marks[bis[n]] = (VE, p.cnt[VE])
            p.tt("vector", dnb4[:, wsl], dnb4[:, wsl], x1r[:, xb + wsl.start:
                 xb + wsl.stop], ALU.add,
                 inc=(VE if sub == 3 else None))
        ve_x1[t % 2] = p.cnt[VE]
        if t + 1 < n_tiles:
            b4_loads(t + 1, ve_x1[(t + 1) % 2], te_end[(t + 1) % 2])
        te_end[t % 2] = p.cnt[TE]

        for w in range(4):
            p.red("vector", lnst[:, w:w + 1], dnb4[:, w * DM:(w + 1) * DM],
                  ALU.add)
        p.nopinc("vector", VE)
        p.wait("scalar", VE, p.cnt[VE])
        p.wait("scalar", STO, None)
        for w in range(4):
            p.act("scalar", ost[:, w * DM:(w + 1) * DM],
                  dnb4[:, w * DM:(w + 1) * DM], ACT.Square,
                  accum=lnst[:, 4 + w:5 + w], inc=AC)
        p.wait("vector", AC, p.cnt[AC])
        p.ts("vector", lnst[:, 8:12], lnst[:, 0:4], 1.0 / DM, None, ALU.mult)
        p.ts("vector", lnst[:, 12:16], lnst[:, 4:8], 1.0 / DM, None, ALU.mult)
        p.tt("vector", lnst[:, 16:20], lnst[:, 8:12], lnst[:, 8:12], ALU.mult)
        p.tt("vector", lnst[:, 20:24], lnst[:, 12:16], lnst[:, 16:20],
             ALU.subtract, inc=VE)
        p.wait("scalar", VE, p.cnt[VE])
        p.act("scalar", lnst[:, 24:28], lnst[:, 20:24], ACT.Ln,
              bias=epscol[:, 0:1])
        p.act("scalar", lnst[:, 28:32], lnst[:, 24:28], ACT.Exp,
              scale=nhalf[:, 0:1], inc=AC)
        p.wait("vector", AC, p.cnt[AC])
        for w in range(4):
            p.ts("vector", ost[:, w * DM:(w + 1) * DM],
                 dnb4[:, w * DM:(w + 1) * DM], lnst[:, 8 + w:9 + w],
                 lnst[:, 28 + w:29 + w], ALU.subtract, ALU.mult)
        p.nopinc("vector", VE)
        p.wait("gpsimd", VE, None)
        for w in range(4):
            p.dma("gpsimd", OUT[T0 + w * 128:T0 + w * 128 + 128, :],
                  ost[:, w * DM:(w + 1) * DM], STO)

    p.wait("sync", STO, None)
    p.flush()
    ph4.close()
    phx.close()
    ctx.close()
    return nc


_CACHE = {}


def kernel(L0_attn_x, L0_attn_W_in, L0_attn_conv_w, L0_attn_conv_b,
           L0_attn_W_xproj, L0_attn_A_log, L0_attn_D_param, L0_attn_W_out,
           L0_ffn_w_gate, L0_ffn_w_up, L0_ffn_w_down):
    x = np.asarray(L0_attn_x, np.float32)
    f32 = lambda a: np.ascontiguousarray(np.asarray(a, np.float32))

    xpad = np.pad(x, ((0, 0), (3, 0), (0, 0)))
    xt = np.ascontiguousarray(xpad.transpose(0, 2, 1))

    w_inT = f32(np.asarray(L0_attn_W_in).T)
    wxT = f32(np.asarray(L0_attn_W_xproj).T)
    w_outT = f32(np.asarray(L0_attn_W_out).T)
    wguT = f32(np.concatenate([np.asarray(L0_ffn_w_gate).T,
                               np.asarray(L0_ffn_w_up).T], axis=1))
    wdT = f32(np.asarray(L0_ffn_w_down).T)
    convw = f32(np.asarray(L0_attn_conv_w).reshape(16, 128, 4)
                .transpose(1, 0, 2).reshape(128, 64))
    convb = f32(np.asarray(L0_attn_conv_b).reshape(16, 128).T)
    dparam = f32(np.asarray(L0_attn_D_param).reshape(16, 128).T)
    alog48 = f32(np.tile(np.asarray(L0_attn_A_log), 3)[:, None])

    if "nc" not in _CACHE:
        _CACHE["nc"] = build_nc()
    nc = _CACHE["nc"]
    in_maps = _stage(x, xt, w_inT, wxT, w_outT, wguT, wdT, convw, convb,
                     dparam, alog48)
    res = run_bass_kernel_spmd(nc, in_maps, core_ids=list(range(8)))
    out = np.empty((B, L, DM), np.float32)
    for k in range(8):
        b, q = k // 4, k % 4
        out[b, q * TOK:(q + 1) * TOK] = np.asarray(res.results[k]["out"])
    return out


def _stage(x, xt, w_inT, wxT, w_outT, wguT, wdT, convw, convb, dparam,
           alog48):
    mask = np.zeros((CH, AUGK), np.float32)
    for tau in range(CH):
        mask[tau, tau:CH] = 536870912.0
    mask[:, CH:] = 536870912.0
    ident16 = np.eye(16, dtype=np.float32)
    in_maps = []
    for k in range(8):
        b, q = k // 4, k % 4
        mh = [1.0 if j < q else 0.0 for j in range(3)]
        mp = [1.0 if (0 < j < q) else 0.0 for j in range(3)]
        cmb = np.tile(np.array(mh + mp + [1.0 - v for v in mp], np.float32),
                      (16, 1))
        in_maps.append({
            "xt": np.ascontiguousarray(xt[b][:, q * TOK: q * TOK + TOK + 3]),
            "w_inT": w_inT, "wxT": wxT, "convw": convw, "convb": convb,
            "dparam": dparam, "alog48": alog48, "w_outT": w_outT,
            "wguT": wguT, "wdT": wdT, "mask": mask, "ident16": ident16,
            "cmb": np.ascontiguousarray(cmb),
        })
    return in_maps
